# version 01
# baseline (speedup 1.0000x reference)
import math
import contextlib
import numpy as np
import ml_dtypes

import concourse.bass as bass
import concourse.mybir as mybir
from concourse.bass_utils import run_bass_kernel_spmd

F32 = mybir.dt.float32
BF16 = mybir.dt.bfloat16
AF = mybir.ActivationFunctionType
ALU = mybir.AluOpType
AX = mybir.AxisListType

B, S, DM = 4, 4096, 2048
NHEAD, NKV, HD = 16, 4, 128
DH = 8192
EPS = 1e-5
ROPE_BASE = 1e6

T = 2048
DC = DM // 128
NQ = 4
TQ = T // NQ
HC = DH // 128
G = 4
HCG = HC // G
OC = (NHEAD + 2 * NKV) * HD // 128
NB = T // 128
SCL = 1.0 / math.sqrt(HD)
NEG = -1e9

PHASE_LIMIT = 99
DEBUG_FFN = False
DEBUG_ATT = False


class Sched:
    ENGINES = ("sync", "tensor", "vector", "scalar", "gpsimd")

    def __init__(self, nc):
        self.nc = nc
        self.q = {e: [] for e in self.ENGINES}
        self.sem = {}
        self.cnt = {}
        self.stack = contextlib.ExitStack()

    def new_sem(self, name):
        self.sem[name] = self.stack.enter_context(self.nc.semaphore(name))
        self.cnt[name] = 0

    def op(self, eng, f, wait=None, inc=None, inc16=None):
        waits = [(self.sem[s], v) for (s, v) in (wait or []) if v > 0]
        incs = []
        for s in ([inc] if isinstance(inc, str) else (inc or [])):
            self.cnt[s] += 1
            incs.append((self.sem[s], 1))
        for s in ([inc16] if isinstance(inc16, str) else (inc16 or [])):
            self.cnt[s] += 16
            incs.append((self.sem[s], 16))

        def emit(e, waits=waits, incs=incs, f=f):
            for (h, v) in waits:
                e.wait_ge(h, v)
            ins = f(e)
            for (h, k) in incs:
                ins.then_inc(h, k)
        self.q[eng].append(emit)

    def wait(self, eng, s, v=None):
        v = self.cnt[s] if v is None else v
        h = self.sem[s]
        if v > 0:
            self.q[eng].append(lambda e, h=h, v=v: e.wait_ge(h, v))

    def barrier(self):
        if "bar" not in self.sem:
            self.new_sem("bar")
        tgt = self.cnt["bar"] + len(self.ENGINES)
        self.cnt["bar"] = tgt
        h = self.sem["bar"]
        for eng in self.ENGINES:
            def emit(e, h=h, tgt=tgt):
                e.nop().then_inc(h, 1)
                e.wait_ge(h, tgt)
            self.q[eng].append(emit)

    def flush(self, block):
        for eng in self.ENGINES:
            ops = self.q[eng]
            if not ops:
                continue
            self.q[eng] = []

            def run(e, ops=ops):
                for f in ops:
                    f(e)
            getattr(block, eng)(run)


def build_program():
    nc = bass.Bass("TRN2", target_bir_lowering=False)

    def din(name, shape, dt=F32):
        return nc.dram_tensor(name, shape, dt, kind="ExternalInput").ap()

    x_fm = din("x_fm", [DM, T])
    w_qkv = din("w_qkv", [(NHEAD + 2 * NKV) * HD, DM])
    w_o = din("w_o", [DM, NHEAD * HD])
    qn_g, qn_b = din("qn_g", [HD, 1]), din("qn_b", [HD, 1])
    kn_g, kn_b = din("kn_g", [HD, 1]), din("kn_b", [HD, 1])
    w_gate = din("w_gate", [DH, DM])
    w_up = din("w_up", [DH, DM])
    w_down = din("w_down", [DM, DH])
    cos_fm = din("cos_fm", [HD, T])
    sin_fm = din("sin_fm", [HD, T])
    qpos_in = din("qpos", [128, NB])
    kpos_in = din("kpos", [128, 256])
    qsh_in = din("qsh", [128, 1])
    ident_bf_in = din("ident_bf_i", [128, 128], BF16)
    ident_f_in = din("ident_f_i", [128, 128], F32)

    out_fm = nc.dram_tensor("out_fm", [DM, T], F32, kind="ExternalOutput").ap()
    if DEBUG_FFN:
        dbg_xln = nc.dram_tensor("dbg_xln", [128, DC * TQ], BF16, kind="ExternalOutput").ap()
        dbg_gated = nc.dram_tensor("dbg_gated", [128, HCG * TQ], BF16, kind="ExternalOutput").ap()
    else:
        dbg_xln = dbg_gated = None
    if DEBUG_ATT:
        dbg_S = nc.dram_tensor("dbg_S", [128, 512], F32, kind="ExternalOutput").ap()
        dbg_P = nc.dram_tensor("dbg_P", [128, 512], BF16, kind="ExternalOutput").ap()
        dbg_m = nc.dram_tensor("dbg_m", [128, 24], F32, kind="ExternalOutput").ap()
        dbg_av = nc.dram_tensor("dbg_av", [128, 128], BF16, kind="ExternalOutput").ap()
        dbg_K = nc.dram_tensor("dbg_K", [128, 512], BF16, kind="ExternalOutput").ap()
        dbg_Q = nc.dram_tensor("dbg_Q", [128, 256], BF16, kind="ExternalOutput").ap()
    else:
        dbg_S = dbg_P = dbg_m = dbg_av = dbg_K = dbg_Q = None

    def dint(name, shape, dt=BF16):
        return nc.dram_tensor(name, shape, dt, kind="Internal").ap()

    wgT_d = dint("wgT_d", [HC, 128, DM])
    wuT_d = dint("wuT_d", [HC, 128, DM])
    wdT_d = dint("wdT_d", [DC, G, 128, 2048])
    qkvT_d = dint("qkvT_d", [OC, 128, DM])
    woT_d = dint("woT_d", [NHEAD * HD, DM])
    q_d = dint("q_d", [NHEAD, HD, T])
    kex_in = dint("kex_in", [NKV * HD, T])
    kex_out = dint("kex_out", [2 * NKV * HD, T])
    vex_in = dint("vex_in", [NKV * T, HD])
    vex_out = dint("vex_out", [2 * NKV * T, HD])

    st = contextlib.ExitStack()
    e = st.enter_context

    xs = e(nc.sbuf_tensor("xs", [128, DC * T], F32))
    ident_b = e(nc.sbuf_tensor("ident_b", [128, 128], BF16))
    ident_f = e(nc.sbuf_tensor("ident_f", [128, 128], F32))
    ones_b = e(nc.sbuf_tensor("ones_b", [128, 1], BF16))
    ones_f = e(nc.sbuf_tensor("ones_f", [1, 128], F32))
    qpos_s = e(nc.sbuf_tensor("qpos_s", [128, NB], F32))
    qsh_s = e(nc.sbuf_tensor("qsh_s", [128, 1], F32))
    qng_s = e(nc.sbuf_tensor("qng_s", [128, 1], F32))
    qnb_s = e(nc.sbuf_tensor("qnb_s", [128, 1], F32))
    kng_s = e(nc.sbuf_tensor("kng_s", [128, 1], F32))
    knb_s = e(nc.sbuf_tensor("knb_s", [128, 1], F32))

    sc = Sched(nc)
    for s_ in ("ld", "wld", "store", "pe", "act", "dve", "pool", "cc", "swp", "boot"):
        sc.new_sem(s_)

    block = e(nc.Block())

    def xs_t(dc):
        return xs[:, dc * T:(dc + 1) * T]

    BOOT = 1 << 24
    all_sems = [sc.sem[s_] for s_ in ("ld", "wld", "store", "pe", "act", "dve",
                                      "pool", "cc", "swp")]

    sc.new_sem("bar")

    def _boot_clear(e_):
        for h in all_sems:
            e_.sem_clear(h)
        e_.sem_clear(sc.sem["bar"])
        e_.sem_clear(sc.sem["boot"])
        return e_.sem_inc(sc.sem["boot"], BOOT)
    sc.q["sync"].append(lambda e_: _boot_clear(e_))
    for eng in ("tensor", "vector", "scalar", "gpsimd"):
        sc.q[eng].append(lambda e_: e_.wait_ge(sc.sem["boot"], BOOT))

    for dc in range(DC):
        sc.op("sync", lambda e_, dc=dc: e_.dma_start(
            out=xs_t(dc), in_=x_fm[dc * 128:(dc + 1) * 128, :]), inc16="ld")
    for dst, src in ((ident_b, ident_bf_in), (ident_f, ident_f_in), (qpos_s, qpos_in),
                     (qsh_s, qsh_in),
                     (qng_s, qn_g), (qnb_s, qn_b), (kng_s, kn_g), (knb_s, kn_b)):
        sc.op("sync", lambda e_, d=dst, s_=src: e_.dma_start(out=d[:], in_=s_[:, :]), inc16="ld")
    ld_p0 = sc.cnt["ld"]
    sc.op("gpsimd", lambda e_: e_.memset(ones_b[:], 1.0), inc="pool")
    sc.op("gpsimd", lambda e_: e_.memset(ones_f[:], 1.0), inc="pool")
    sc.op("vector", lambda e_: e_.tensor_scalar_mul(qng_s[:], qng_s[:], SCL),
          wait=[("ld", ld_p0)], inc="dve")
    sc.op("vector", lambda e_: e_.tensor_scalar_mul(qnb_s[:], qnb_s[:], SCL), inc="dve")

    with nc.sbuf_tensor("wrm", [128, 128], F32) as wrm, \
         nc.sbuf_tensor("wrmb", [128, 128], BF16) as wrmb, \
         nc.psum_tensor("ps_wrm", [128, 128], F32) as ps_wrm:
        sc.op("gpsimd", lambda e_: e_.iota(wrm[:, 0:128], [[1, 128]], base=0,
                                           channel_multiplier=0,
                                           allow_small_or_imprecise_dtypes=True),
              inc="pool")
        sc.op("gpsimd", lambda e_: e_.tensor_copy(wrmb[:], wrm[:]),
              wait=[("pool", sc.cnt["pool"])], inc="pool")
        sc.op("gpsimd", lambda e_: e_.memset(wrm[:], 1.0),
              wait=[("pool", sc.cnt["pool"])], inc="pool")
        p_w = sc.cnt["pool"]
        for fn in (AF.Exp, AF.Silu, AF.Ln, AF.Square, AF.Copy):
            sc.op("scalar", lambda e_, fn=fn: e_.activation(wrm[:], wrm[:], fn),
                  wait=[("pool", p_w)], inc="act")
        sc.op("vector", lambda e_: e_.reduce_max(wrm[:, 0:1], wrm[:], axis=AX.X,
                                                 negate=True),
              wait=[("act", sc.cnt["act"])], inc="dve")
        sc.op("vector", lambda e_: e_.reciprocal(wrm[:, 1:2], wrm[:, 0:1]), inc="dve")
        sc.op("tensor", lambda e_: e_.matmul(ps_wrm[:], wrmb[:], wrmb[:],
                                             start=True, stop=True),
              wait=[("pool", p_w), ("dve", sc.cnt["dve"])], inc="pe")
        sc.barrier()
        sc.flush(block)

    with nc.sbuf_tensor("prep_nat", [128, DH], F32) as prep_nat, \
         nc.sbuf_tensor("prep_stg", [128, 2048], BF16) as prep_stg, \
         nc.sbuf_tensor("prep_stg2", [128, 2048], BF16) as prep_stg2, \
         nc.psum_tensor("ps_pr1", [128, 512], F32) as ps_pr1, \
         nc.psum_tensor("ps_pr2", [128, 512], F32) as ps_pr2:

        pspr = [ps_pr1, ps_pr2]
        stgs = [prep_stg, prep_stg2]
        freed = {"nat": 0, "stg0": 0, "stg1": 0, "ps0": 0, "ps1": 0}

        def prep_weight(w_ap, O, I, store_fn, nat_cols):
            n_span = max(1, I // 2048)
            span_w = min(I, 2048)
            for r in range(O // 128):
                for piece in range(I // nat_cols):
                    c0 = piece * nat_cols
                    sc.op("sync", lambda e_, r=r, c0=c0, w=nat_cols:
                          e_.dma_start(out=prep_nat[:, c0:c0 + w],
                                       in_=w_ap[r * 128:(r + 1) * 128, c0:c0 + w]),
                          wait=[("pe", freed["nat"])] if piece == 0 else None,
                          inc16="wld")
                wld_here = sc.cnt["wld"]
                for sp in range(n_span):
                    sidx = (r * n_span + sp) % 2
                    stg = stgs[sidx]
                    for quad in range(span_w // 512):
                        pidx = quad % 2
                        ps = pspr[pidx]
                        for j in range(4):
                            ic = quad * 4 + j
                            col = sp * 2048 + ic * 128
                            w_ = []
                            if quad == 0 and j == 0 and sp == 0:
                                w_.append(("wld", wld_here))
                            if j == 0:
                                w_.append(("act", freed[f"ps{pidx}"]))
                            sc.op("tensor", lambda e_, ps=ps, j=j, col=col:
                                  e_.transpose(ps[:, j * 128:(j + 1) * 128],
                                               prep_nat[:, col:col + 128], ident_f[:]),
                                  wait=w_, inc="pe" if j == 3 else None)
                        pe_q = sc.cnt["pe"]
                        sc.op("scalar", lambda e_, ps=ps, stg=stg, quad=quad:
                              e_.copy(stg[:, quad * 512:(quad + 1) * 512], ps[:]),
                              wait=[("pe", pe_q), ("store", freed[f"stg{sidx}"])] if quad == 0
                              else [("pe", pe_q)],
                              inc="act")
                        freed[f"ps{pidx}"] = sc.cnt["act"]
                    store_fn(r, sp, stg)
                    freed[f"stg{sidx}"] = sc.cnt["store"]
                freed["nat"] = sc.cnt["pe"]

        def mk_simple_store(dst):
            def store(r, sp, stg):
                sc.op("sync", lambda e_, r=r, stg=stg:
                      e_.dma_start(out=dst[r], in_=stg[:, :]),
                      wait=[("act", sc.cnt["act"])], inc16="store")
            return store

        def wd_store(r, sp, stg):
            sc.op("sync", lambda e_, r=r, sp=sp, stg=stg:
                  e_.dma_start(out=wdT_d[r, sp], in_=stg[:, :]),
                  wait=[("act", sc.cnt["act"])], inc16="store")

        wo_view = woT_d.rearrange("(ec p) c -> ec p c", p=128)

        def wo_store(r, sp, stg):
            a_ = sc.cnt["act"]
            for ec in range(16):
                sc.op("scalar", lambda e_, r=r, ec=ec, stg=stg:
                      e_.dma_start(out=wo_view[ec, :, r * 128:(r + 1) * 128],
                                   in_=stg[:, ec * 128:(ec + 1) * 128]),
                      wait=[("act", a_)] if ec == 0 else None, inc16="store")

        if PHASE_LIMIT >= 1:
            prep_weight(w_gate, DH, DM, mk_simple_store(wgT_d), 2048)
            prep_weight(w_up, DH, DM, mk_simple_store(wuT_d), 2048)
            prep_weight(w_down, DM, DH, wd_store, 4096)
        if PHASE_LIMIT >= 2:
            prep_weight(w_qkv, OC * 128, DM, mk_simple_store(qkvT_d), 2048)
            prep_weight(w_o, DM, NHEAD * HD, wo_store, 2048)

        sc.barrier()
        sc.flush(block)

    def emit_stream_ln(t0, xln_dst, xcast, xsq, tmp, vslices, ps_a, ps_b, ps_s1, ps_s2):
        V_MU, V_M2, V_VAR, V_RSTD, V_NMR = vslices
        pe_after = {}
        for dc in range(DC):
            sl = (dc % 2) * TQ
            xc = xcast[:, sl:sl + TQ]
            xq_ = xsq[:, sl:sl + TQ]
            w0 = [("pe", sc.cnt["pe"]), ("dve", sc.cnt["dve"])] if dc == 0                 else ([("pe", pe_after[dc - 2])] if dc >= 2 else None)
            sc.op("gpsimd", lambda e_, dc=dc, xc=xc: e_.tensor_copy(xc, xs_t(dc)[:, t0:t0 + TQ]),
                  wait=w0, inc="pool")
            wsq = [("pe", sc.cnt["pe"])] if dc == 0                 else ([("pe", pe_after[dc - 2])] if dc >= 2 else None)
            sc.op("scalar", lambda e_, dc=dc, xq_=xq_: e_.activation(xq_, xs_t(dc)[:, t0:t0 + TQ],
                                                                     AF.Square),
                  wait=wsq, inc="act")
            sc.op("tensor", lambda e_, dc=dc, xc=xc: e_.matmul(ps_s1[:], ones_b[:], xc,
                                                               start=(dc == 0), stop=(dc == DC - 1)),
                  wait=[("pool", sc.cnt["pool"])] + ([("dve", sc.cnt["dve"])] if dc == 0 else []))
            sc.op("tensor", lambda e_, dc=dc, xq_=xq_: e_.matmul(ps_s2[:], ones_b[:], xq_,
                                                                 start=(dc == 0), stop=(dc == DC - 1)),
                  wait=[("act", sc.cnt["act"])], inc="pe")
            pe_after[dc] = sc.cnt["pe"]
        pe_stats = sc.cnt["pe"]
        sc.op("vector", lambda e_: e_.tensor_scalar_mul(V_MU, ps_s1[:], 1.0 / DM),
              wait=[("pe", pe_stats)], inc="dve")
        sc.op("vector", lambda e_: e_.tensor_scalar_mul(V_M2, ps_s2[:], 1.0 / DM), inc="dve")
        sc.op("vector", lambda e_: e_.tensor_mul(V_VAR, V_MU, V_MU), inc="dve")
        sc.op("vector", lambda e_: e_.tensor_sub(V_VAR, V_M2, V_VAR), inc="dve")
        sc.op("vector", lambda e_: e_.tensor_scalar_add(V_VAR, V_VAR, EPS), inc="dve")
        sc.op("scalar", lambda e_: e_.activation(V_RSTD, V_VAR, AF.Ln),
              wait=[("dve", sc.cnt["dve"])], inc="act")
        sc.op("scalar", lambda e_: e_.activation(V_RSTD, V_RSTD, AF.Exp, scale=-0.5), inc="act")
        a_st = sc.cnt["act"]
        sc.op("vector", lambda e_: e_.tensor_mul(V_NMR, V_MU, V_RSTD),
              wait=[("act", a_st)], inc="dve")
        sc.op("vector", lambda e_: e_.tensor_scalar_mul(V_NMR, V_NMR, -1.0), inc="dve")
        d_st = sc.cnt["dve"]
        sc.op("tensor", lambda e_: e_.matmul(ps_a[:], ones_f[:], V_RSTD, start=True, stop=True),
              wait=[("dve", d_st), ("act", sc.cnt["act"])])
        sc.op("tensor", lambda e_: e_.matmul(ps_b[:], ones_f[:], V_NMR, start=True, stop=True),
              inc="pe")
        pe_bc = sc.cnt["pe"]
        for dc in range(DC):
            sc.op("vector", lambda e_, dc=dc: e_.tensor_mul(tmp[:], xs_t(dc)[:, t0:t0 + TQ], ps_a[:]),
                  wait=[("pe", pe_bc)] if dc == 0 else None, inc="dve")
            sc.op("vector", lambda e_, dc=dc: e_.tensor_add(xln_dst[:, dc * TQ:(dc + 1) * TQ],
                                                            tmp[:], ps_b[:]), inc="dve")
        return sc.cnt["dve"]

    def emit_ffn(tag):
        with contextlib.ExitStack() as fst:
            fe = fst.enter_context
            xln = fe(nc.sbuf_tensor(f"xln{tag}", [128, DC * TQ], BF16))
            gatedT = fe(nc.sbuf_tensor(f"gatedT{tag}", [128, HCG * TQ], BF16))
            wg_t = fe(nc.sbuf_tensor(f"wg_t{tag}", [128, 2 * DM], BF16))
            wu_t = fe(nc.sbuf_tensor(f"wu_t{tag}", [128, 2 * DM], BF16))
            wd_t = fe(nc.sbuf_tensor(f"wd_t{tag}", [128, 2 * 2048], BF16))
            xcast = fe(nc.sbuf_tensor(f"xc{tag}", [128, 2 * TQ], BF16))
            xsq = fe(nc.sbuf_tensor(f"xq{tag}", [128, 2 * TQ], BF16))
            tmp = fe(nc.sbuf_tensor(f"tmp{tag}", [128, TQ], F32))
            gate_s = fe(nc.sbuf_tensor(f"gs{tag}", [128, TQ], F32))
            up_s = fe(nc.sbuf_tensor(f"us{tag}", [128, TQ], F32))
            vec = fe(nc.sbuf_tensor(f"vec{tag}", [1, 5 * TQ], F32))
            ps_s1 = fe(nc.psum_tensor(f"ps_s1{tag}", [1, TQ], F32))
            ps_s2 = fe(nc.psum_tensor(f"ps_s2{tag}", [1, TQ], F32))
            ps_g = fe(nc.psum_tensor(f"ps_g{tag}", [128, TQ], F32))
            ps_u = fe(nc.psum_tensor(f"ps_u{tag}", [128, TQ], F32))
            ps_d1 = fe(nc.psum_tensor(f"ps_d1{tag}", [128, TQ], F32))
            ps_d2 = fe(nc.psum_tensor(f"ps_d2{tag}", [128, TQ], F32))

            ps_d = [ps_d1, ps_d2]
            vsl = tuple(vec[:, i * TQ:(i + 1) * TQ] for i in range(5))
            freed = {"wg0": 0, "wg1": 0, "wu0": 0, "wu1": 0, "wd0": 0, "wd1": 0,
                     "gate_ev": 0, "up_ev": 0, "dps0": 0, "dps1": 0}

            for q in range(NQ):
                t0 = q * TQ
                dve_xln = emit_stream_ln(t0, xln, xcast, xsq, tmp, vsl,
                                         ps_g, ps_u, ps_s1, ps_s2)
                if DEBUG_FFN and tag == "f1" and q == 0:
                    sc.op("scalar", lambda e_: e_.dma_start(out=dbg_xln[:, :], in_=xln[:]),
                          wait=[("dve", dve_xln)], inc16="store")
                for g in range(G):
                    for hcg in range(HCG):
                        hc = g * HCG + hcg
                        bf = hc % 2
                        sc.op("sync", lambda e_, hc=hc, bf=bf:
                              e_.dma_start(out=wg_t[:, bf * DM:(bf + 1) * DM], in_=wgT_d[hc]),
                              wait=[("pe", freed[f"wg{bf}"])], inc16="wld")
                        sc.op("sync", lambda e_, hc=hc, bf=bf:
                              e_.dma_start(out=wu_t[:, bf * DM:(bf + 1) * DM], in_=wuT_d[hc]),
                              wait=[("pe", freed[f"wu{bf}"])], inc16="wld")
                        wld_hc = sc.cnt["wld"]
                        for dc in range(DC):
                            w_ = None
                            if dc == 0:
                                w_ = [("wld", wld_hc), ("dve", dve_xln),
                                      ("act", freed["gate_ev"])]
                            sc.op("tensor", lambda e_, dc=dc, bf=bf:
                                  e_.matmul(ps_g[:],
                                            wg_t[:, bf * DM + dc * 128: bf * DM + (dc + 1) * 128],
                                            xln[:, dc * TQ:(dc + 1) * TQ],
                                            start=(dc == 0), stop=(dc == DC - 1)),
                                  wait=w_, inc="pe" if dc == DC - 1 else None)
                        pe_gate = sc.cnt["pe"]
                        freed[f"wg{bf}"] = pe_gate
                        for dc in range(DC):
                            w_ = [("act", freed["up_ev"])] if dc == 0 else None
                            sc.op("tensor", lambda e_, dc=dc, bf=bf:
                                  e_.matmul(ps_u[:],
                                            wu_t[:, bf * DM + dc * 128: bf * DM + (dc + 1) * 128],
                                            xln[:, dc * TQ:(dc + 1) * TQ],
                                            start=(dc == 0), stop=(dc == DC - 1)),
                                  wait=w_, inc="pe" if dc == DC - 1 else None)
                        pe_up = sc.cnt["pe"]
                        freed[f"wu{bf}"] = pe_up
                        dve_gprev = sc.cnt["dve"]
                        sc.op("scalar", lambda e_: e_.activation(gate_s[:], ps_g[:], AF.Silu),
                              wait=[("pe", pe_gate), ("dve", dve_gprev)], inc="act")
                        freed["gate_ev"] = sc.cnt["act"]
                        sc.op("scalar", lambda e_: e_.copy(up_s[:], ps_u[:]),
                              wait=[("pe", pe_up)], inc="act")
                        freed["up_ev"] = sc.cnt["act"]
                        act_ev = sc.cnt["act"]
                        mw = [("act", act_ev), ("pe", sc.cnt["pe"])]
                        if DEBUG_FFN and tag == "f1" and q == 0 and g == 1 and hcg == 0:
                            mw.append(("store", sc.cnt["store"]))
                        sc.op("vector", lambda e_, hcg=hcg:
                              e_.tensor_mul(gatedT[:, hcg * TQ:(hcg + 1) * TQ], gate_s[:], up_s[:]),
                              wait=mw, inc="dve")
                    dve_gated = sc.cnt["dve"]
                    if DEBUG_FFN and tag == "f1" and q == 0 and g == 0:
                        sc.op("scalar", lambda e_: e_.dma_start(out=dbg_gated[:, :],
                                                                in_=gatedT[:]),
                              wait=[("dve", dve_gated)], inc16="store")
                        dbg_store = sc.cnt["store"]
                    for dco in range(DC):
                        bf = dco % 2
                        sc.op("sync", lambda e_, dco=dco, g=g, bf=bf:
                              e_.dma_start(out=wd_t[:, bf * 2048:(bf + 1) * 2048],
                                           in_=wdT_d[dco, g]),
                              wait=[("pe", freed[f"wd{bf}"])], inc16="wld")
                        wld_wd = sc.cnt["wld"]
                        ps = ps_d[dco % 2]
                        for hcg in range(HCG):
                            w_ = None
                            if hcg == 0:
                                w_ = [("wld", wld_wd), ("dve", dve_gated),
                                      ("dve", freed[f"dps{dco % 2}"])]
                            sc.op("tensor", lambda e_, hcg=hcg, bf=bf, ps=ps:
                                  e_.matmul(ps[:],
                                            wd_t[:, bf * 2048 + hcg * 128: bf * 2048 + (hcg + 1) * 128],
                                            gatedT[:, hcg * TQ:(hcg + 1) * TQ],
                                            start=(hcg == 0), stop=(hcg == HCG - 1)),
                                  wait=w_, inc="pe" if hcg == HCG - 1 else None)
                        pe_down = sc.cnt["pe"]
                        freed[f"wd{bf}"] = pe_down
                        sc.op("vector", lambda e_, dco=dco, ps=ps, t0=t0:
                              e_.scalar_tensor_tensor(xs_t(dco)[:, t0:t0 + TQ], ps[:], 0.5,
                                                      xs_t(dco)[:, t0:t0 + TQ], ALU.mult, ALU.add),
                              wait=[("pe", pe_down)], inc="dve")
                        freed[f"dps{dco % 2}"] = sc.cnt["dve"]
        sc.barrier()
        sc.flush(block)

    if PHASE_LIMIT >= 1:
        emit_ffn("f1")

    store_qkv_end = 0
    if PHASE_LIMIT >= 2:
        with contextlib.ExitStack() as qst:
            qe = qst.enter_context
            xln2 = qe(nc.sbuf_tensor("xln2", [128, DC * TQ], BF16))
            cos_s = qe(nc.sbuf_tensor("cos_s", [128, T], F32))
            sin_s = qe(nc.sbuf_tensor("sin_s", [128, T], F32))
            qkv_t = qe(nc.sbuf_tensor("qkv_t", [128, 2 * DM], BF16))
            xcast2 = qe(nc.sbuf_tensor("xc2", [128, 2 * TQ], BF16))
            xsq2 = qe(nc.sbuf_tensor("xq2", [128, 2 * TQ], BF16))
            tmp2 = qe(nc.sbuf_tensor("tmp2", [128, TQ], F32))
            qf = qe(nc.sbuf_tensor("qf", [128, TQ], F32))
            qb2 = qe(nc.sbuf_tensor("qb2", [128, TQ], BF16))
            qq2 = qe(nc.sbuf_tensor("qq2", [128, TQ], BF16))
            qn_t = qe(nc.sbuf_tensor("qn_t", [128, TQ], F32))
            qr1 = qe(nc.sbuf_tensor("qr1", [128, TQ], F32))
            qr2 = qe(nc.sbuf_tensor("qr2", [128, TQ], F32))
            qro = qe(nc.sbuf_tensor("qro", [128, TQ], BF16))
            qsw = qe(nc.sbuf_tensor("qsw", [128, TQ], F32))
            vtm = qe(nc.sbuf_tensor("vtm", [128, TQ], BF16))
            vec2 = qe(nc.sbuf_tensor("vec2", [1, 5 * TQ], F32))
            ps_q1 = qe(nc.psum_tensor("ps_q1", [128, TQ], F32))
            ps_q2 = qe(nc.psum_tensor("ps_q2", [128, TQ], F32))
            ps_t1 = qe(nc.psum_tensor("ps_t1", [1, TQ], F32))
            ps_t2 = qe(nc.psum_tensor("ps_t2", [1, TQ], F32))
            ps_b1 = qe(nc.psum_tensor("ps_b1", [128, TQ], F32))
            ps_b2 = qe(nc.psum_tensor("ps_b2", [128, TQ], F32))
            ps_vt = qe(nc.psum_tensor("ps_vt", [128, TQ], BF16))

            vsl2 = tuple(vec2[:, i * TQ:(i + 1) * TQ] for i in range(5))
            V2_MU, V2_M2, V2_VAR, V2_RSTD, V2_NMR = vsl2
            ps_q = [ps_q1, ps_q2]
            freed = {"qkv0": 0, "qkv1": 0, "qev0": 0, "qev1": 0}

            sc.op("sync", lambda e_: e_.dma_start(out=cos_s[:], in_=cos_fm[:, :]), inc16="ld")
            sc.op("sync", lambda e_: e_.dma_start(out=sin_s[:], in_=sin_fm[:, :]), inc16="ld")
            ld_rope = sc.cnt["ld"]

            for q in range(NQ):
                t0 = q * TQ
                dve_xln2 = emit_stream_ln(t0, xln2, xcast2, xsq2, tmp2, vsl2,
                                          ps_b1, ps_b2, ps_t1, ps_t2)
                for oc in range(OC):
                    bf = oc % 2
                    ps = ps_q[oc % 2]
                    sc.op("sync", lambda e_, oc=oc, bf=bf:
                          e_.dma_start(out=qkv_t[:, bf * DM:(bf + 1) * DM], in_=qkvT_d[oc]),
                          wait=[("pe", freed[f"qkv{bf}"])], inc16="wld")
                    wld_oc = sc.cnt["wld"]
                    for dc in range(DC):
                        w_ = None
                        if dc == 0:
                            w_ = [("wld", wld_oc), ("dve", dve_xln2),
                                  ("act", freed[f"qev{oc % 2}"])]
                        sc.op("tensor", lambda e_, dc=dc, bf=bf, ps=ps:
                              e_.matmul(ps[:],
                                        qkv_t[:, bf * DM + dc * 128: bf * DM + (dc + 1) * 128],
                                        xln2[:, dc * TQ:(dc + 1) * TQ],
                                        start=(dc == 0), stop=(dc == DC - 1)),
                              wait=w_, inc="pe" if dc == DC - 1 else None)
                    pe_qkv = sc.cnt["pe"]
                    freed[f"qkv{bf}"] = pe_qkv

                    if oc < NHEAD + NKV:
                        is_q = oc < NHEAD
                        gg, bb = (qng_s, qnb_s) if is_q else (kng_s, knb_s)
                        dve_pre = sc.cnt["dve"]
                        sc.op("scalar", lambda e_, ps=ps: e_.copy(qf[:], ps[:]),
                              wait=[("pe", pe_qkv), ("dve", dve_pre),
                                    ("pool", sc.cnt["pool"])], inc="act")
                        freed[f"qev{oc % 2}"] = sc.cnt["act"]
                        act_e = sc.cnt["act"]
                        sc.op("gpsimd", lambda e_: e_.tensor_copy(qb2[:], qf[:]),
                              wait=[("act", act_e), ("pe", sc.cnt["pe"])], inc="pool")
                        sc.op("scalar", lambda e_: e_.activation(qq2[:], qf[:], AF.Square),
                              inc="act")
                        sc.op("tensor", lambda e_: e_.matmul(ps_t1[:], ones_b[:], qb2[:],
                                                             start=True, stop=True),
                              wait=[("pool", sc.cnt["pool"]), ("dve", sc.cnt["dve"])])
                        sc.op("tensor", lambda e_: e_.matmul(ps_t2[:], ones_b[:], qq2[:],
                                                             start=True, stop=True),
                              wait=[("act", sc.cnt["act"])], inc="pe")
                        pe_st = sc.cnt["pe"]
                        sc.op("vector", lambda e_: e_.tensor_scalar_mul(V2_MU, ps_t1[:], 1.0 / HD),
                              wait=[("pe", pe_st)], inc="dve")
                        sc.op("vector", lambda e_: e_.tensor_scalar_mul(V2_M2, ps_t2[:], 1.0 / HD),
                              inc="dve")
                        sc.op("vector", lambda e_: e_.tensor_mul(V2_VAR, V2_MU, V2_MU), inc="dve")
                        sc.op("vector", lambda e_: e_.tensor_sub(V2_VAR, V2_M2, V2_VAR), inc="dve")
                        sc.op("vector", lambda e_: e_.tensor_scalar_add(V2_VAR, V2_VAR, EPS),
                              inc="dve")
                        sc.op("scalar", lambda e_: e_.activation(V2_RSTD, V2_VAR, AF.Ln),
                              wait=[("dve", sc.cnt["dve"])], inc="act")
                        sc.op("scalar", lambda e_: e_.activation(V2_RSTD, V2_RSTD, AF.Exp, scale=-0.5),
                              inc="act")
                        a_s = sc.cnt["act"]
                        sc.op("vector", lambda e_: e_.tensor_mul(V2_NMR, V2_MU, V2_RSTD),
                              wait=[("act", a_s)], inc="dve")
                        sc.op("vector", lambda e_: e_.tensor_scalar_mul(V2_NMR, V2_NMR, -1.0),
                              inc="dve")
                        d_s = sc.cnt["dve"]
                        sc.op("tensor", lambda e_: e_.matmul(ps_b1[:], ones_f[:], V2_RSTD,
                                                             start=True, stop=True),
                              wait=[("dve", d_s)])
                        sc.op("tensor", lambda e_: e_.matmul(ps_b2[:], ones_f[:], V2_NMR,
                                                             start=True, stop=True), inc="pe")
                        pe_b = sc.cnt["pe"]
                        sc.op("vector", lambda e_: e_.tensor_mul(tmp2[:], qf[:], ps_b1[:]),
                              wait=[("pe", pe_b)], inc="dve")
                        sc.op("vector", lambda e_: e_.tensor_add(tmp2[:], tmp2[:], ps_b2[:]),
                              inc="dve")
                        sc.op("vector", lambda e_, gg=gg, bb=bb:
                              e_.tensor_scalar(qn_t[:], tmp2[:], gg[:], bb[:],
                                               ALU.mult, ALU.add),
                              inc="dve")
                        dve_qn = sc.cnt["dve"]
                        sc.op("scalar", lambda e_: e_.dma_start(out=qsw[0:64, :],
                                                                in_=qn_t[64:128, :]),
                              wait=[("dve", dve_qn)], inc16="swp")
                        sc.op("scalar", lambda e_: e_.dma_start(out=qsw[64:128, :],
                                                                in_=qn_t[0:64, :]),
                              inc16="swp")
                        swp_ro = sc.cnt["swp"]
                        sc.op("vector", lambda e_, t0=t0:
                              e_.tensor_mul(qr1[:], qn_t[:], cos_s[:, t0:t0 + TQ]),
                              wait=[("ld", ld_rope)], inc="dve")
                        sc.op("vector", lambda e_, t0=t0:
                              e_.tensor_mul(qr2[:], qsw[:], sin_s[:, t0:t0 + TQ]),
                              wait=[("swp", swp_ro)], inc="dve")
                        sc.op("vector", lambda e_: e_.tensor_add(qro[:], qr1[:], qr2[:]),
                              wait=[("store", sc.cnt["store"])], inc="dve")
                        dve_ro = sc.cnt["dve"]
                        if is_q:
                            sc.op("scalar", lambda e_, oc=oc, t0=t0:
                                  e_.dma_start(out=q_d[oc, :, t0:t0 + TQ], in_=qro[:]),
                                  wait=[("dve", dve_ro)], inc16="store")
                        else:
                            kv = oc - NHEAD
                            sc.op("scalar", lambda e_, kv=kv, t0=t0:
                                  e_.dma_start(out=kex_in[kv * HD:(kv + 1) * HD, t0:t0 + TQ],
                                               in_=qro[:]),
                                  wait=[("dve", dve_ro)], inc16="store")
                    else:
                        kv = oc - NHEAD - NKV
                        dve_pre = sc.cnt["dve"]
                        sc.op("scalar", lambda e_, ps=ps: e_.copy(vtm[:], ps[:]),
                              wait=[("pe", pe_qkv), ("dve", dve_pre), ("store", sc.cnt["store"])],
                              inc="act")
                        freed[f"qev{oc % 2}"] = sc.cnt["act"]
                        act_v = sc.cnt["act"]
                        for j in range(4):
                            sc.op("tensor", lambda e_, j=j:
                                  e_.transpose(ps_vt[:, j * 128:(j + 1) * 128],
                                               vtm[:, j * 128:(j + 1) * 128], ident_b[:]),
                                  wait=[("act", act_v), ("dve", sc.cnt["dve"])] if j == 0 else None,
                                  inc="pe" if j == 3 else None)
                        pe_vt = sc.cnt["pe"]
                        sc.op("vector", lambda e_: e_.tensor_copy(vtm[:], ps_vt[:]),
                              wait=[("pe", pe_vt), ("act", sc.cnt["act"])], inc="dve")
                        dve_v = sc.cnt["dve"]
                        sc.op("scalar", lambda e_, kv=kv, t0=t0:
                              e_.dma_start(
                                  out=vex_in.rearrange("(kv tc p) e -> kv p tc e", kv=NKV, p=128)
                                      [kv][:, t0 // 128: t0 // 128 + 4, :],
                                  in_=vtm.rearrange("p (tc e) -> p tc e", e=128)),
                              wait=[("dve", dve_v)], inc16="store")
            store_qkv_end = sc.cnt["store"]
            sc.barrier()
            sc.flush(block)

    if PHASE_LIMIT >= 3:
        sc.wait("gpsimd", "store", store_qkv_end)
        sc.op("gpsimd", lambda e_: e_.collective_compute(
            "AllGather", ALU.bypass,
            replica_groups=[[0, 1], [2, 3], [4, 5], [6, 7]],
            ins=[kex_in[:, :]], outs=[kex_out[:, :]]), inc="cc")
        sc.op("gpsimd", lambda e_: e_.collective_compute(
            "AllGather", ALU.bypass,
            replica_groups=[[0, 1], [2, 3], [4, 5], [6, 7]],
            ins=[vex_in[:, :]], outs=[vex_out[:, :]]), inc="cc")
        sc.flush(block)

        with contextlib.ExitStack() as ast_:
            ae = ast_.enter_context
            KVg = ae(nc.sbuf_tensor("KVg", [128, 4 * T], BF16))
            Ssb = ae(nc.sbuf_tensor("Ssb", [128, 2 * T], F32))
            Psb = ae(nc.sbuf_tensor("Psb", [128, 2 * 512], BF16))
            PTs = ae(nc.sbuf_tensor("PTs", [128, 2 * 512], BF16))
            avT = ae(nc.sbuf_tensor("avT", [128, 4 * T], BF16))
            woT_s = ae(nc.sbuf_tensor("woT_s", [128, 4 * DM], BF16))
            Qh = ae(nc.sbuf_tensor("Qh", [128, T], BF16))
            avsb = ae(nc.sbuf_tensor("avsb", [128, 128], BF16))
            mrow = ae(nc.sbuf_tensor("mrow", [128, 24], F32))
            mask_t = ae(nc.sbuf_tensor("mask_t", [128, 256], F32))
            kpos_t = ae(nc.sbuf_tensor("kpos_t", [128, 256], F32))
            qpos_sh = ae(nc.sbuf_tensor("qpos_sh", [128, 1], F32))
            ps_S1 = ae(nc.psum_tensor("ps_S1", [128, 512], F32))
            ps_S2 = ae(nc.psum_tensor("ps_S2", [128, 512], F32))
            ps_PT = ae(nc.psum_tensor("ps_PT", [128, 512], BF16))
            ps_AV = ae(nc.psum_tensor("ps_AV", [128, 128], F32))
            ps_AT = ae(nc.psum_tensor("ps_AT", [128, 128], BF16))
            ps_W1 = ae(nc.psum_tensor("ps_W1", [128, TQ], F32))
            ps_W2 = ae(nc.psum_tensor("ps_W2", [128, TQ], F32))

            Kp = KVg[:, 0:2 * T]
            Vp = KVg[:, 2 * T:4 * T]
            ps_S = [ps_S1, ps_S2]
            ps_W = [ps_W1, ps_W2]
            NM, NM2, DN, RC = mrow[:, 0:1], mrow[:, 3:4], mrow[:, 1:2], mrow[:, 2:3]
            DNs = mrow[:, 4:12]
            SPC = mrow[:, 20:21]
            SPC2 = mrow[:, 21:22]

            sc.op("sync", lambda e_: e_.dma_start(out=kpos_t[:], in_=kpos_in[:, :]),
                  inc16="ld")
            ld_kpos = sc.cnt["ld"]
            sc.op("vector", lambda e_:
                  e_.tensor_scalar(mask_t[:], kpos_t[:], qsh_s[:], float(NEG),
                                   ALU.is_gt, ALU.mult),
                  wait=[("ld", ld_kpos)], inc="dve")
            dve_mask = sc.cnt["dve"]
            sc.op("gpsimd", lambda e_: e_.memset(mrow[:], 0.0), inc="pool")
            sc.op("gpsimd", lambda e_: e_.memset(Ssb[:], 0.0), inc="pool")
            sc.op("gpsimd", lambda e_: e_.memset(Psb[:], 0.0), inc="pool")
            pool_kpos = sc.cnt["pool"]
            sc.wait("scalar", "pool", pool_kpos)
            sc.wait("vector", "pool", pool_kpos)

            kex_v = kex_out.rearrange("(half kv p) t -> half kv p t", half=2, p=128)
            vex_v = vex_out.rearrange("(half kv tc p) e -> half kv tc p e",
                                      half=2, kv=NKV, p=128)
            freed = {"kv": 0, "Q": 0, "wo": 0, "wops0": 0, "wops1": 0, "avT": 0,
                     "S0": 0, "S1": 0, "pw0": 0, "pw1": 0, "pts0": 0, "pts1": 0}

            for g in range(NKV):
                kvw = [("cc", 1), ("pe", freed["kv"]), ("store", store_qkv_end)]
                for j in range(16):
                    sc.op("sync", lambda e_, g=g, j=j: e_.dma_start(
                        out=Kp.rearrange("p (j half k) -> p j half k", half=2, k=128)[:, j],
                        in_=kex_v[:, g].rearrange("half p (j k) -> p j half k", k=128)[:, j],
                        allow_slow_non_contiguous=True),
                        wait=kvw if j == 0 else None, inc16="ld")
                for tc in range(16):
                    sc.op("sync", lambda e_, g=g, tc=tc: e_.dma_start(
                        out=Vp.rearrange("p (tc half e) -> p tc half e", half=2, e=128)[:, tc],
                        in_=vex_v[:, g].rearrange("half tc p e -> p tc half e")[:, tc],
                        allow_slow_non_contiguous=True),
                        wait=[("cc", 2)] if tc == 0 else None, inc16="ld")
                for j in range(4):
                    sc.op("sync", lambda e_, g=g, j=j: e_.dma_start(
                        out=woT_s[:, j * DM:(j + 1) * DM],
                        in_=woT_d[(4 * g + j) * 128:(4 * g + j + 1) * 128, :]),
                        wait=[("pe", freed["wo"])] if j == 0 else None,
                        inc16="ld")
                ld_g = sc.cnt["ld"]

                for hh in range(4):
                    h = 4 * g + hh
                    sc.op("sync", lambda e_, h=h: e_.dma_start(out=Qh[:], in_=q_d[h]),
                          wait=[("pe", freed["Q"]), ("store", store_qkv_end)],
                          inc16="ld")
                    ld_q = sc.cnt["ld"]
                    for tb in range(NB):
                        W = 256 * (tb + 1)
                        nblk = (W + 511) // 512
                        for b_ in range(nblk):
                            w0 = b_ * 512
                            w1 = min(W, w0 + 512)
                            ps = ps_S[b_ % 2]
                            w_ = [("act", freed[f"S{b_ % 2}"])]
                            if b_ == 0:
                                w_.append(("ld", ld_q))
                            sc.op("tensor", lambda e_, ps=ps, w0=w0, w1=w1, tb=tb:
                                  e_.matmul(ps[:, 0:w1 - w0],
                                            Qh[:, tb * 128:(tb + 1) * 128],
                                            Kp[:, w0:w1], start=True, stop=True),
                                  wait=w_, inc="pe")
                            pe_S = sc.cnt["pe"]
                            sc.op("scalar", lambda e_, ps=ps, w0=w0, w1=w1:
                                  e_.copy(Ssb[:, w0:w1], ps[:, 0:w1 - w0]),
                                  wait=[("pe", pe_S)], inc="act")
                            freed[f"S{b_ % 2}"] = sc.cnt["act"]
                        act_evac = sc.cnt["act"]
                        sc.op("vector", lambda e_, W=W:
                              e_.tensor_add(Ssb[:, W - 256:W], Ssb[:, W - 256:W], mask_t[:]),
                              wait=[("act", act_evac), ("dve", dve_mask),
                                    ("pool", pool_kpos)], inc="dve")
                        sc.op("vector", lambda e_, W=W:
                              e_.reduce_max(NM, Ssb[:, 0:W], axis=AX.X, negate=True),
                              inc="dve")
                        sc.op("vector", lambda e_: e_.tensor_scalar_add(SPC, qsh_s[:], 0.0),
                              inc="dve")
                        dve_m = sc.cnt["dve"]
                        for b_ in range(nblk):
                            w0 = b_ * 512
                            w1 = min(W, w0 + 512)
                            pw = b_ % 2
                            pwin = Psb[:, pw * 512: pw * 512 + (w1 - w0)]
                            sc.op("scalar", lambda e_, pwin=pwin, w0=w0, w1=w1, b_=b_:
                                  e_.activation(pwin, Ssb[:, w0:w1], AF.Exp, bias=NM,
                                                accum_out=DNs[:, b_:b_ + 1]),
                                  wait=[("dve", dve_m), ("pe", freed[f"pw{pw}"])],
                                  inc="act")
                            act_exp = sc.cnt["act"]
                            nkc_s = (w1 - w0) // 128
                            for i_ in range(nkc_s):
                                w_ = None
                                if i_ == 0:
                                    w_ = [("act", act_exp), ("dve", freed.get("ptps", 0))]
                                sc.op("tensor", lambda e_, i_=i_, pw=pw:
                                      e_.transpose(ps_PT[:, i_ * 128:(i_ + 1) * 128],
                                                   Psb[:, pw * 512 + i_ * 128: pw * 512 + (i_ + 1) * 128],
                                                   ident_b[:]),
                                      wait=w_, inc="pe" if i_ == nkc_s - 1 else None)
                            freed[f"pw{pw}"] = sc.cnt["pe"]
                            pe_pt = sc.cnt["pe"]
                            pts = PTs[:, pw * 512: pw * 512 + 512]
                            sc.op("vector", lambda e_, pts=pts, n=nkc_s:
                                  e_.tensor_copy(pts[:, 0:n * 128], ps_PT[:, 0:n * 128]),
                                  wait=[("pe", pe_pt)], inc="dve")
                            freed["ptps"] = sc.cnt["dve"]
                            dve_pt = sc.cnt["dve"]
                            for i_ in range(nkc_s):
                                kc = b_ * 4 + i_
                                w_ = None
                                if i_ == 0:
                                    w_ = [("dve", dve_pt)]
                                    if kc == 0:
                                        w_.append(("dve", freed.get("av_ev", 0)))
                                sc.op("tensor", lambda e_, pts=pts, i_=i_, kc=kc,
                                      first=(kc == 0), last=(kc == W // 128 - 1):
                                      e_.matmul(ps_AV[:], pts[:, i_ * 128:(i_ + 1) * 128],
                                                Vp[:, kc * 128:(kc + 1) * 128],
                                                start=first, stop=last, skip_group_check=True),
                                      wait=w_, inc="pe" if i_ == nkc_s - 1 else None)
                        pe_av = sc.cnt["pe"]
                        sc.op("scalar", lambda e_: e_.copy(SPC2, qsh_s[:]), inc="act")
                        sc.op("vector", lambda e_, n=nblk:
                              e_.reduce_sum(DN, DNs[:, 0:n], axis=AX.X),
                              wait=[("act", sc.cnt["act"])], inc="dve")
                        sc.op("vector", lambda e_: e_.tensor_scalar_add(SPC, qsh_s[:], 0.0),
                              inc="dve")
                        sc.op("vector", lambda e_: e_.reciprocal(RC, DN), inc="dve")
                        sc.op("vector", lambda e_: e_.tensor_scalar_add(SPC, qsh_s[:], 0.0),
                              inc="dve")
                        dve_rc = sc.cnt["dve"]
                        sc.op("vector", lambda e_:
                              e_.tensor_scalar_mul(avsb[:], ps_AV[:], RC),
                              wait=[("pe", pe_av)], inc="dve")
                        act_av = sc.cnt["act"]
                        dve_av = sc.cnt["dve"]
                        freed["av_ev"] = dve_av
                        if DEBUG_ATT and g == 0 and hh == 0 and tb == 0:
                            for dst, srcap in ((dbg_S, Ssb[:, 0:512]),
                                               (dbg_P, Psb[:, 0:512]),
                                               (dbg_m, mrow[:]),
                                               (dbg_av, avsb[:]),
                                               (dbg_K, Kp[:, 0:512]),
                                               (dbg_Q, Qh[:, 0:256])):
                                sc.op("gpsimd", lambda e_, d=dst, s_=srcap:
                                      e_.dma_start(out=d[:, :], in_=s_),
                                      wait=[("dve", dve_av)], inc16="store")
                            sc.wait("scalar", "store", sc.cnt["store"])
                            sc.wait("vector", "store", sc.cnt["store"])
                            sc.wait("tensor", "store", sc.cnt["store"])
                        sc.op("tensor", lambda e_: e_.transpose(ps_AT[:], avsb[:], ident_b[:]),
                              wait=[("dve", dve_av), ("dve", freed["avT"])], inc="pe")
                        pe_at = sc.cnt["pe"]
                        sc.op("vector", lambda e_, hh=hh, tb=tb:
                              e_.tensor_copy(avT[:, hh * T + tb * 128: hh * T + (tb + 1) * 128],
                                             ps_AT[:]),
                              wait=[("pe", pe_at)], inc="dve")
                        freed["avT"] = sc.cnt["dve"]
                    freed["Q"] = sc.cnt["pe"]
                dve_avT = sc.cnt["dve"]
                for dco in range(DC):
                    for tm in range(NQ):
                        i = dco * NQ + tm
                        ps = ps_W[i % 2]
                        for j in range(4):
                            w_ = None
                            if j == 0:
                                w_ = [("dve", dve_avT), ("dve", freed[f"wops{i % 2}"]),
                                      ("ld", ld_g)]
                            sc.op("tensor", lambda e_, ps=ps, j=j, dco=dco, tm=tm:
                                  e_.matmul(ps[:],
                                            woT_s[:, j * DM + dco * 128: j * DM + (dco + 1) * 128],
                                            avT[:, j * T + tm * TQ: j * T + (tm + 1) * TQ],
                                            start=(j == 0), stop=(j == 3)),
                                  wait=w_, inc="pe" if j == 3 else None)
                        pe_wo = sc.cnt["pe"]
                        sc.op("vector", lambda e_, ps=ps, dco=dco, tm=tm:
                              e_.tensor_add(xs_t(dco)[:, tm * TQ:(tm + 1) * TQ],
                                            xs_t(dco)[:, tm * TQ:(tm + 1) * TQ], ps[:]),
                              wait=[("pe", pe_wo)], inc="dve")
                        freed[f"wops{i % 2}"] = sc.cnt["dve"]
                freed["kv"] = sc.cnt["pe"]
                freed["wo"] = sc.cnt["pe"]
            sc.barrier()
            sc.flush(block)

    if PHASE_LIMIT >= 4:
        emit_ffn("f3")

    for dc in range(DC):
        w_ = None
        if dc == 0:
            w_ = [("dve", sc.cnt["dve"]), ("pe", sc.cnt["pe"]), ("act", sc.cnt["act"])]
        sc.op("sync", lambda e_, dc=dc: e_.dma_start(
            out=out_fm[dc * 128:(dc + 1) * 128, :], in_=xs_t(dc)),
            wait=w_, inc16="store")
    sc.wait("sync", "store", sc.cnt["store"])
    sc.flush(block)

    st.close()
    sc.stack.close()
    return nc


def _core_positions(s):
    blocks = [(2 * j + s) for j in range(NB)]
    return np.concatenate([np.arange(b * 128, (b + 1) * 128) for b in blocks])


def _rope_tables(pos):
    inv_freq = (1.0 / (ROPE_BASE ** (np.arange(0, HD, 2, dtype=np.float32) / HD))).astype(np.float32)
    freqs = np.outer(pos.astype(np.float32), inv_freq).astype(np.float32)
    cos = np.concatenate([np.cos(freqs), np.cos(freqs)], axis=-1)
    sin = np.concatenate([-np.sin(freqs), np.sin(freqs)], axis=-1)
    return (np.ascontiguousarray(cos.T.astype(np.float32)),
            np.ascontiguousarray(sin.T.astype(np.float32)))


def kernel(L0_attn_x, L0_attn_w_qkv, L0_attn_w_o, L0_attn_qn_g, L0_attn_qn_b,
           L0_attn_kn_g, L0_attn_kn_b, L0_ffn_w_gate, L0_ffn_w_up, L0_ffn_w_down):
    x = np.asarray(L0_attn_x, dtype=np.float32)
    ident = np.eye(128, dtype=np.float32)
    in_common = {
        "w_qkv": np.ascontiguousarray(np.asarray(L0_attn_w_qkv, np.float32)),
        "w_o": np.ascontiguousarray(np.asarray(L0_attn_w_o, np.float32)),
        "qn_g": np.asarray(L0_attn_qn_g, np.float32).reshape(HD, 1),
        "qn_b": np.asarray(L0_attn_qn_b, np.float32).reshape(HD, 1),
        "kn_g": np.asarray(L0_attn_kn_g, np.float32).reshape(HD, 1),
        "kn_b": np.asarray(L0_attn_kn_b, np.float32).reshape(HD, 1),
        "w_gate": np.ascontiguousarray(np.asarray(L0_ffn_w_gate, np.float32)),
        "w_up": np.ascontiguousarray(np.asarray(L0_ffn_w_up, np.float32)),
        "w_down": np.ascontiguousarray(np.asarray(L0_ffn_w_down, np.float32)),
        "ident_bf_i": ident.astype(ml_dtypes.bfloat16),
        "ident_f_i": ident,
        "kpos": np.ascontiguousarray(np.tile(np.arange(256, dtype=np.float32), (128, 1))),
    }
    in_maps = []
    poss = []
    for c in range(8):
        b, s = c // 2, c % 2
        pos = _core_positions(s)
        poss.append(pos)
        cos_t, sin_t = _rope_tables(pos)
        m = dict(in_common)
        m["x_fm"] = np.ascontiguousarray(x[b][pos].T)
        m["cos_fm"] = cos_t
        m["sin_fm"] = sin_t
        m["qpos"] = np.ascontiguousarray(pos.astype(np.float32).reshape(NB, 128).T)
        m["qsh"] = np.ascontiguousarray((128.0 * s + np.arange(128, dtype=np.float32)).reshape(128, 1))
        in_maps.append(m)

    nc = build_program()
    res = run_bass_kernel_spmd(nc, in_maps, core_ids=list(range(8)))

    out = np.empty((B, S, DM), dtype=np.float32)
    for c in range(8):
        b = c // 2
        out[b][poss[c]] = np.asarray(res.results[c]["out_fm"]).T
    return out


# version 04 (final)
# speedup vs baseline: 1.1977x; 1.1977x over previous
import math
import contextlib
import numpy as np
import ml_dtypes

import concourse.bass as bass
import concourse.mybir as mybir
from concourse.bass_utils import run_bass_kernel_spmd

F32 = mybir.dt.float32
BF16 = mybir.dt.bfloat16
AF = mybir.ActivationFunctionType
ALU = mybir.AluOpType
AX = mybir.AxisListType

B, S, DM = 4, 4096, 2048
NHEAD, NKV, HD = 16, 4, 128
DH = 8192
EPS = 1e-5
ROPE_BASE = 1e6

T = 2048
DC = DM // 128
NQ = 4
TQ = T // NQ
HC = DH // 128
G = 4
HCG = HC // G
OC = (NHEAD + 2 * NKV) * HD // 128
NB = T // 128
SCL = 1.0 / math.sqrt(HD)
NEG = -1e9

PHASE_LIMIT = 99
ATT_WO = True
ATT_UNITS = None
DEBUG_FFN = False
DEBUG_ATT = False


class Sched:
    ENGINES = ("sync", "tensor", "vector", "scalar", "gpsimd")

    def __init__(self, nc):
        self.nc = nc
        self.q = {e: [] for e in self.ENGINES}
        self.sem = {}
        self.cnt = {}
        self.stack = contextlib.ExitStack()

    def new_sem(self, name):
        self.sem[name] = self.stack.enter_context(self.nc.semaphore(name))
        self.cnt[name] = 0

    def op(self, eng, f, wait=None, inc=None, inc16=None):
        waits = [(self.sem[s], v) for (s, v) in (wait or []) if v > 0]
        incs = []
        for s in ([inc] if isinstance(inc, str) else (inc or [])):
            self.cnt[s] += 1
            incs.append((self.sem[s], 1))
        for s in ([inc16] if isinstance(inc16, str) else (inc16 or [])):
            self.cnt[s] += 16
            incs.append((self.sem[s], 16))

        def emit(e, waits=waits, incs=incs, f=f):
            for (h, v) in waits:
                e.wait_ge(h, v)
            ins = f(e)
            for (h, k) in incs:
                ins.then_inc(h, k)
        self.q[eng].append(emit)

    def wait(self, eng, s, v=None):
        v = self.cnt[s] if v is None else v
        h = self.sem[s]
        if v > 0:
            self.q[eng].append(lambda e, h=h, v=v: e.wait_ge(h, v))

    def barrier(self):
        if "bar" not in self.sem:
            self.new_sem("bar")
        tgt = self.cnt["bar"] + len(self.ENGINES)
        self.cnt["bar"] = tgt
        h = self.sem["bar"]
        for eng in self.ENGINES:
            def emit(e, h=h, tgt=tgt):
                e.nop().then_inc(h, 1)
                e.wait_ge(h, tgt)
            self.q[eng].append(emit)

    def flush(self, block):
        for eng in self.ENGINES:
            ops = self.q[eng]
            if not ops:
                continue
            self.q[eng] = []

            def run(e, ops=ops):
                for f in ops:
                    f(e)
            getattr(block, eng)(run)


def build_program():
    nc = bass.Bass("TRN2", target_bir_lowering=False)

    def din(name, shape, dt=F32):
        return nc.dram_tensor(name, shape, dt, kind="ExternalInput").ap()

    x_fm = din("x_fm", [DM, T])
    w_qkv = din("w_qkv", [(NHEAD + 2 * NKV) * HD, DM])
    w_o = din("w_o", [DM, NHEAD * HD])
    qn_g, qn_b = din("qn_g", [HD, 1]), din("qn_b", [HD, 1])
    kn_g, kn_b = din("kn_g", [HD, 1]), din("kn_b", [HD, 1])
    w_gate = din("w_gate", [DH, DM])
    w_up = din("w_up", [DH, DM])
    w_down = din("w_down", [DM, DH])
    cos_fm = din("cos_fm", [HD, T])
    sin_fm = din("sin_fm", [HD, T])
    qpos_in = din("qpos", [128, NB])
    kpos_in = din("kpos", [128, 256])
    qsh_in = din("qsh", [128, 1])
    ident_bf_in = din("ident_bf_i", [128, 128], BF16)
    ident_f_in = din("ident_f_i", [128, 128], F32)

    out_fm = nc.dram_tensor("out_fm", [DM, T], F32, kind="ExternalOutput").ap()
    if DEBUG_FFN:
        dbg_xln = nc.dram_tensor("dbg_xln", [128, DC * TQ], BF16, kind="ExternalOutput").ap()
        dbg_gated = nc.dram_tensor("dbg_gated", [128, HCG * TQ], BF16, kind="ExternalOutput").ap()
    else:
        dbg_xln = dbg_gated = None
    if DEBUG_ATT:
        dbg_S = nc.dram_tensor("dbg_S", [128, 512], F32, kind="ExternalOutput").ap()
        dbg_P = nc.dram_tensor("dbg_P", [128, 512], BF16, kind="ExternalOutput").ap()
        dbg_m = nc.dram_tensor("dbg_m", [128, 24], F32, kind="ExternalOutput").ap()
        dbg_av = nc.dram_tensor("dbg_av", [128, 128], BF16, kind="ExternalOutput").ap()
        dbg_K = nc.dram_tensor("dbg_K", [128, 512], BF16, kind="ExternalOutput").ap()
        dbg_Q = nc.dram_tensor("dbg_Q", [128, 256], BF16, kind="ExternalOutput").ap()
    else:
        dbg_S = dbg_P = dbg_m = dbg_av = dbg_K = dbg_Q = None

    def dint(name, shape, dt=BF16):
        return nc.dram_tensor(name, shape, dt, kind="Internal").ap()

    wgT_d = dint("wgT_d", [HC, 128, DM])
    wuT_d = dint("wuT_d", [HC, 128, DM])
    wdT_d = dint("wdT_d", [DC, G, 128, 2048])
    qkvT_d = dint("qkvT_d", [OC, 128, DM])
    woT_d = dint("woT_d", [NHEAD * HD, DM])
    q_d = dint("q_d", [NHEAD, HD, T])
    kex_in = dint("kex_in", [NKV * HD, T])
    kex_out = dint("kex_out", [2 * NKV * HD, T])
    vex_in = dint("vex_in", [NKV * T, HD])
    vex_out = dint("vex_out", [2 * NKV * T, HD])

    st = contextlib.ExitStack()
    e = st.enter_context

    xs = e(nc.sbuf_tensor("xs", [128, DC * T], F32))
    ident_b = e(nc.sbuf_tensor("ident_b", [128, 128], BF16))
    ident_f = e(nc.sbuf_tensor("ident_f", [128, 128], F32))
    ones_b = e(nc.sbuf_tensor("ones_b", [128, 1], BF16))
    ones_f = e(nc.sbuf_tensor("ones_f", [1, 128], F32))
    qpos_s = e(nc.sbuf_tensor("qpos_s", [128, NB], F32))
    qsh_s = e(nc.sbuf_tensor("qsh_s", [128, 1], F32))
    qng_s = e(nc.sbuf_tensor("qng_s", [128, 1], F32))
    qnb_s = e(nc.sbuf_tensor("qnb_s", [128, 1], F32))
    kng_s = e(nc.sbuf_tensor("kng_s", [128, 1], F32))
    knb_s = e(nc.sbuf_tensor("knb_s", [128, 1], F32))

    sc = Sched(nc)
    for s_ in ("ld", "wld", "store", "pe", "act", "dve", "pool", "cc", "swp", "boot"):
        sc.new_sem(s_)

    block = e(nc.Block())

    def xs_t(dc):
        return xs[:, dc * T:(dc + 1) * T]

    BOOT = 1 << 24
    all_sems = [sc.sem[s_] for s_ in ("ld", "wld", "store", "pe", "act", "dve",
                                      "pool", "cc", "swp")]

    sc.new_sem("bar")

    def _boot_clear(e_):
        for h in all_sems:
            e_.sem_clear(h)
        e_.sem_clear(sc.sem["bar"])
        e_.sem_clear(sc.sem["boot"])
        return e_.sem_inc(sc.sem["boot"], BOOT)
    sc.q["sync"].append(lambda e_: _boot_clear(e_))
    for eng in ("tensor", "vector", "scalar", "gpsimd"):
        sc.q[eng].append(lambda e_: e_.wait_ge(sc.sem["boot"], BOOT))

    for dc in range(DC):
        sc.op("sync", lambda e_, dc=dc: e_.dma_start(
            out=xs_t(dc), in_=x_fm[dc * 128:(dc + 1) * 128, :]), inc16="ld")
    for dst, src in ((ident_b, ident_bf_in), (ident_f, ident_f_in), (qpos_s, qpos_in),
                     (qsh_s, qsh_in),
                     (qng_s, qn_g), (qnb_s, qn_b), (kng_s, kn_g), (knb_s, kn_b)):
        sc.op("sync", lambda e_, d=dst, s_=src: e_.dma_start(out=d[:], in_=s_[:, :]), inc16="ld")
    ld_p0 = sc.cnt["ld"]
    sc.op("gpsimd", lambda e_: e_.memset(ones_b[:], 1.0), inc="pool")
    sc.op("gpsimd", lambda e_: e_.memset(ones_f[:], 1.0), inc="pool")
    sc.op("vector", lambda e_: e_.tensor_scalar_mul(qng_s[:], qng_s[:], SCL),
          wait=[("ld", ld_p0)], inc="dve")
    sc.op("vector", lambda e_: e_.tensor_scalar_mul(qnb_s[:], qnb_s[:], SCL), inc="dve")

    with nc.sbuf_tensor("wrm", [128, 128], F32) as wrm, \
         nc.sbuf_tensor("wrmb", [128, 128], BF16) as wrmb, \
         nc.psum_tensor("ps_wrm", [128, 128], F32) as ps_wrm:
        sc.op("gpsimd", lambda e_: e_.iota(wrm[:, 0:128], [[1, 128]], base=0,
                                           channel_multiplier=0,
                                           allow_small_or_imprecise_dtypes=True),
              inc="pool")
        sc.op("gpsimd", lambda e_: e_.tensor_copy(wrmb[:], wrm[:]),
              wait=[("pool", sc.cnt["pool"])], inc="pool")
        sc.op("gpsimd", lambda e_: e_.memset(wrm[:], 1.0),
              wait=[("pool", sc.cnt["pool"])], inc="pool")
        p_w = sc.cnt["pool"]
        for fn in (AF.Exp, AF.Silu, AF.Ln, AF.Square, AF.Copy):
            sc.op("scalar", lambda e_, fn=fn: e_.activation(wrm[:], wrm[:], fn),
                  wait=[("pool", p_w)], inc="act")
        sc.op("vector", lambda e_: e_.reduce_max(wrm[:, 0:1], wrm[:], axis=AX.X,
                                                 negate=True),
              wait=[("act", sc.cnt["act"])], inc="dve")
        sc.op("vector", lambda e_: e_.reciprocal(wrm[:, 1:2], wrm[:, 0:1]), inc="dve")
        sc.op("tensor", lambda e_: e_.matmul(ps_wrm[:], wrmb[:], wrmb[:],
                                             start=True, stop=True),
              wait=[("pool", p_w), ("dve", sc.cnt["dve"])], inc="pe")
        sc.barrier()
        sc.flush(block)

    with nc.sbuf_tensor("prep_nat", [128, DH], F32) as prep_nat, \
         nc.sbuf_tensor("prep_stg", [128, 2048], BF16) as prep_stg, \
         nc.sbuf_tensor("prep_stg2", [128, 2048], BF16) as prep_stg2, \
         nc.psum_tensor("ps_pr1", [128, 512], F32) as ps_pr1, \
         nc.psum_tensor("ps_pr2", [128, 512], F32) as ps_pr2:

        pspr = [ps_pr1, ps_pr2]
        stgs = [prep_stg, prep_stg2]
        freed = {"nat": 0, "stg0": 0, "stg1": 0, "ps0": 0, "ps1": 0}

        def prep_weight(w_ap, O, I, store_fn, nat_cols):
            n_span = max(1, I // 2048)
            span_w = min(I, 2048)
            for r in range(O // 128):
                for piece in range(I // nat_cols):
                    c0 = piece * nat_cols
                    sc.op("sync", lambda e_, r=r, c0=c0, w=nat_cols:
                          e_.dma_start(out=prep_nat[:, c0:c0 + w],
                                       in_=w_ap[r * 128:(r + 1) * 128, c0:c0 + w]),
                          wait=[("pe", freed["nat"])] if piece == 0 else None,
                          inc16="wld")
                wld_here = sc.cnt["wld"]
                for sp in range(n_span):
                    sidx = (r * n_span + sp) % 2
                    stg = stgs[sidx]
                    for quad in range(span_w // 512):
                        pidx = quad % 2
                        ps = pspr[pidx]
                        for j in range(4):
                            ic = quad * 4 + j
                            col = sp * 2048 + ic * 128
                            w_ = []
                            if quad == 0 and j == 0 and sp == 0:
                                w_.append(("wld", wld_here))
                            if j == 0:
                                w_.append(("act", freed[f"ps{pidx}"]))
                            sc.op("tensor", lambda e_, ps=ps, j=j, col=col:
                                  e_.transpose(ps[:, j * 128:(j + 1) * 128],
                                               prep_nat[:, col:col + 128], ident_f[:]),
                                  wait=w_, inc="pe" if j == 3 else None)
                        pe_q = sc.cnt["pe"]
                        sc.op("scalar", lambda e_, ps=ps, stg=stg, quad=quad:
                              e_.copy(stg[:, quad * 512:(quad + 1) * 512], ps[:]),
                              wait=[("pe", pe_q), ("store", freed[f"stg{sidx}"])] if quad == 0
                              else [("pe", pe_q)],
                              inc="act")
                        freed[f"ps{pidx}"] = sc.cnt["act"]
                    store_fn(r, sp, stg)
                    freed[f"stg{sidx}"] = sc.cnt["store"]
                freed["nat"] = sc.cnt["pe"]

        def mk_simple_store(dst):
            def store(r, sp, stg):
                sc.op("sync", lambda e_, r=r, stg=stg:
                      e_.dma_start(out=dst[r], in_=stg[:, :]),
                      wait=[("act", sc.cnt["act"])], inc16="store")
            return store

        def wd_store(r, sp, stg):
            sc.op("sync", lambda e_, r=r, sp=sp, stg=stg:
                  e_.dma_start(out=wdT_d[r, sp], in_=stg[:, :]),
                  wait=[("act", sc.cnt["act"])], inc16="store")

        wo_view = woT_d.rearrange("(ec p) c -> ec p c", p=128)

        def wo_store(r, sp, stg):
            a_ = sc.cnt["act"]
            for ec in range(16):
                sc.op("scalar", lambda e_, r=r, ec=ec, stg=stg:
                      e_.dma_start(out=wo_view[ec, :, r * 128:(r + 1) * 128],
                                   in_=stg[:, ec * 128:(ec + 1) * 128]),
                      wait=[("act", a_)] if ec == 0 else None, inc16="store")

        if PHASE_LIMIT >= 1:
            prep_weight(w_gate, DH, DM, mk_simple_store(wgT_d), 2048)
            prep_weight(w_up, DH, DM, mk_simple_store(wuT_d), 2048)
            prep_weight(w_down, DM, DH, wd_store, 4096)
        if PHASE_LIMIT >= 2:
            prep_weight(w_qkv, OC * 128, DM, mk_simple_store(qkvT_d), 2048)
            prep_weight(w_o, DM, NHEAD * HD, wo_store, 2048)

        sc.barrier()
        sc.flush(block)

    def emit_stream_ln(t0, xln_dst, xcast, xsq, tmp, vslices, ps_a, ps_b, ps_s1, ps_s2):
        V_MU, V_M2, V_VAR, V_RSTD, V_NMR = vslices
        pe_after = {}
        for dc in range(DC):
            sl = (dc % 2) * TQ
            xc = xcast[:, sl:sl + TQ]
            xq_ = xsq[:, sl:sl + TQ]
            w0 = [("pe", sc.cnt["pe"]), ("dve", sc.cnt["dve"])] if dc == 0                 else ([("pe", pe_after[dc - 2])] if dc >= 2 else None)
            sc.op("gpsimd", lambda e_, dc=dc, xc=xc: e_.tensor_copy(xc, xs_t(dc)[:, t0:t0 + TQ]),
                  wait=w0, inc="pool")
            wsq = [("pe", sc.cnt["pe"])] if dc == 0                 else ([("pe", pe_after[dc - 2])] if dc >= 2 else None)
            sc.op("scalar", lambda e_, dc=dc, xq_=xq_: e_.activation(xq_, xs_t(dc)[:, t0:t0 + TQ],
                                                                     AF.Square),
                  wait=wsq, inc="act")
            sc.op("tensor", lambda e_, dc=dc, xc=xc: e_.matmul(ps_s1[:], ones_b[:], xc,
                                                               start=(dc == 0), stop=(dc == DC - 1)),
                  wait=[("pool", sc.cnt["pool"])] + ([("dve", sc.cnt["dve"])] if dc == 0 else []))
            sc.op("tensor", lambda e_, dc=dc, xq_=xq_: e_.matmul(ps_s2[:], ones_b[:], xq_,
                                                                 start=(dc == 0), stop=(dc == DC - 1)),
                  wait=[("act", sc.cnt["act"])], inc="pe")
            pe_after[dc] = sc.cnt["pe"]
        pe_stats = sc.cnt["pe"]
        sc.op("vector", lambda e_: e_.tensor_scalar_mul(V_MU, ps_s1[:], 1.0 / DM),
              wait=[("pe", pe_stats)], inc="dve")
        sc.op("vector", lambda e_: e_.tensor_scalar_mul(V_M2, ps_s2[:], 1.0 / DM), inc="dve")
        sc.op("vector", lambda e_: e_.tensor_mul(V_VAR, V_MU, V_MU), inc="dve")
        sc.op("vector", lambda e_: e_.tensor_sub(V_VAR, V_M2, V_VAR), inc="dve")
        sc.op("vector", lambda e_: e_.tensor_scalar_add(V_VAR, V_VAR, EPS), inc="dve")
        sc.op("scalar", lambda e_: e_.activation(V_RSTD, V_VAR, AF.Ln),
              wait=[("dve", sc.cnt["dve"])], inc="act")
        sc.op("scalar", lambda e_: e_.activation(V_RSTD, V_RSTD, AF.Exp, scale=-0.5), inc="act")
        a_st = sc.cnt["act"]
        sc.op("vector", lambda e_: e_.tensor_mul(V_NMR, V_MU, V_RSTD),
              wait=[("act", a_st)], inc="dve")
        sc.op("vector", lambda e_: e_.tensor_scalar_mul(V_NMR, V_NMR, -1.0), inc="dve")
        d_st = sc.cnt["dve"]
        sc.op("tensor", lambda e_: e_.matmul(ps_a[:], ones_f[:], V_RSTD, start=True, stop=True),
              wait=[("dve", d_st), ("act", sc.cnt["act"])])
        sc.op("tensor", lambda e_: e_.matmul(ps_b[:], ones_f[:], V_NMR, start=True, stop=True),
              inc="pe")
        pe_bc = sc.cnt["pe"]
        for dc in range(DC):
            sc.op("vector", lambda e_, dc=dc: e_.tensor_mul(tmp[:], xs_t(dc)[:, t0:t0 + TQ], ps_a[:]),
                  wait=[("pe", pe_bc)] if dc == 0 else None, inc="dve")
            sc.op("vector", lambda e_, dc=dc: e_.tensor_add(xln_dst[:, dc * TQ:(dc + 1) * TQ],
                                                            tmp[:], ps_b[:]), inc="dve")
        return sc.cnt["dve"]

    def emit_ffn(tag):
        with contextlib.ExitStack() as fst:
            fe = fst.enter_context
            xln = fe(nc.sbuf_tensor(f"xln{tag}", [128, DC * TQ], BF16))
            gatedT = fe(nc.sbuf_tensor(f"gatedT{tag}", [128, HCG * TQ], BF16))
            wg_t = fe(nc.sbuf_tensor(f"wg_t{tag}", [128, 2 * DM], BF16))
            wu_t = fe(nc.sbuf_tensor(f"wu_t{tag}", [128, 2 * DM], BF16))
            wd_t = fe(nc.sbuf_tensor(f"wd_t{tag}", [128, 2 * 2048], BF16))
            xcast = fe(nc.sbuf_tensor(f"xc{tag}", [128, 2 * TQ], BF16))
            xsq = fe(nc.sbuf_tensor(f"xq{tag}", [128, 2 * TQ], BF16))
            tmp = fe(nc.sbuf_tensor(f"tmp{tag}", [128, TQ], F32))
            gate_s = fe(nc.sbuf_tensor(f"gs{tag}", [128, TQ], F32))
            up_s = fe(nc.sbuf_tensor(f"us{tag}", [128, TQ], F32))
            vec = fe(nc.sbuf_tensor(f"vec{tag}", [1, 5 * TQ], F32))
            ps_s1 = fe(nc.psum_tensor(f"ps_s1{tag}", [1, TQ], F32))
            ps_s2 = fe(nc.psum_tensor(f"ps_s2{tag}", [1, TQ], F32))
            ps_g = fe(nc.psum_tensor(f"ps_g{tag}", [128, TQ], F32))
            ps_u = fe(nc.psum_tensor(f"ps_u{tag}", [128, TQ], F32))
            ps_d1 = fe(nc.psum_tensor(f"ps_d1{tag}", [128, TQ], F32))
            ps_d2 = fe(nc.psum_tensor(f"ps_d2{tag}", [128, TQ], F32))

            ps_d = [ps_d1, ps_d2]
            vsl = tuple(vec[:, i * TQ:(i + 1) * TQ] for i in range(5))
            freed = {"wg0": 0, "wg1": 0, "wu0": 0, "wu1": 0, "wd0": 0, "wd1": 0,
                     "gate_ev": 0, "up_ev": 0, "dps0": 0, "dps1": 0}

            for q in range(NQ):
                t0 = q * TQ
                dve_xln = emit_stream_ln(t0, xln, xcast, xsq, tmp, vsl,
                                         ps_g, ps_u, ps_s1, ps_s2)
                if DEBUG_FFN and tag == "f1" and q == 0:
                    sc.op("scalar", lambda e_: e_.dma_start(out=dbg_xln[:, :], in_=xln[:]),
                          wait=[("dve", dve_xln)], inc16="store")
                for g in range(G):
                    for hcg in range(HCG):
                        hc = g * HCG + hcg
                        bf = hc % 2
                        sc.op("sync", lambda e_, hc=hc, bf=bf:
                              e_.dma_start(out=wg_t[:, bf * DM:(bf + 1) * DM], in_=wgT_d[hc]),
                              wait=[("pe", freed[f"wg{bf}"])], inc16="wld")
                        sc.op("sync", lambda e_, hc=hc, bf=bf:
                              e_.dma_start(out=wu_t[:, bf * DM:(bf + 1) * DM], in_=wuT_d[hc]),
                              wait=[("pe", freed[f"wu{bf}"])], inc16="wld")
                        wld_hc = sc.cnt["wld"]
                        for dc in range(DC):
                            w_ = None
                            if dc == 0:
                                w_ = [("wld", wld_hc), ("dve", dve_xln),
                                      ("act", freed["gate_ev"])]
                            sc.op("tensor", lambda e_, dc=dc, bf=bf:
                                  e_.matmul(ps_g[:],
                                            wg_t[:, bf * DM + dc * 128: bf * DM + (dc + 1) * 128],
                                            xln[:, dc * TQ:(dc + 1) * TQ],
                                            start=(dc == 0), stop=(dc == DC - 1)),
                                  wait=w_, inc="pe" if dc == DC - 1 else None)
                        pe_gate = sc.cnt["pe"]
                        freed[f"wg{bf}"] = pe_gate
                        for dc in range(DC):
                            w_ = [("act", freed["up_ev"])] if dc == 0 else None
                            sc.op("tensor", lambda e_, dc=dc, bf=bf:
                                  e_.matmul(ps_u[:],
                                            wu_t[:, bf * DM + dc * 128: bf * DM + (dc + 1) * 128],
                                            xln[:, dc * TQ:(dc + 1) * TQ],
                                            start=(dc == 0), stop=(dc == DC - 1)),
                                  wait=w_, inc="pe" if dc == DC - 1 else None)
                        pe_up = sc.cnt["pe"]
                        freed[f"wu{bf}"] = pe_up
                        dve_gprev = sc.cnt["dve"]
                        sc.op("scalar", lambda e_: e_.activation(gate_s[:], ps_g[:], AF.Silu),
                              wait=[("pe", pe_gate), ("dve", dve_gprev)], inc="act")
                        freed["gate_ev"] = sc.cnt["act"]
                        sc.op("scalar", lambda e_: e_.copy(up_s[:], ps_u[:]),
                              wait=[("pe", pe_up)], inc="act")
                        freed["up_ev"] = sc.cnt["act"]
                        act_ev = sc.cnt["act"]
                        mw = [("act", act_ev), ("pe", sc.cnt["pe"])]
                        if DEBUG_FFN and tag == "f1" and q == 0 and g == 1 and hcg == 0:
                            mw.append(("store", sc.cnt["store"]))
                        sc.op("vector", lambda e_, hcg=hcg:
                              e_.tensor_mul(gatedT[:, hcg * TQ:(hcg + 1) * TQ], gate_s[:], up_s[:]),
                              wait=mw, inc="dve")
                    dve_gated = sc.cnt["dve"]
                    if DEBUG_FFN and tag == "f1" and q == 0 and g == 0:
                        sc.op("scalar", lambda e_: e_.dma_start(out=dbg_gated[:, :],
                                                                in_=gatedT[:]),
                              wait=[("dve", dve_gated)], inc16="store")
                        dbg_store = sc.cnt["store"]
                    for dco in range(DC):
                        bf = dco % 2
                        sc.op("sync", lambda e_, dco=dco, g=g, bf=bf:
                              e_.dma_start(out=wd_t[:, bf * 2048:(bf + 1) * 2048],
                                           in_=wdT_d[dco, g]),
                              wait=[("pe", freed[f"wd{bf}"])], inc16="wld")
                        wld_wd = sc.cnt["wld"]
                        ps = ps_d[dco % 2]
                        for hcg in range(HCG):
                            w_ = None
                            if hcg == 0:
                                w_ = [("wld", wld_wd), ("dve", dve_gated),
                                      ("dve", freed[f"dps{dco % 2}"])]
                            sc.op("tensor", lambda e_, hcg=hcg, bf=bf, ps=ps:
                                  e_.matmul(ps[:],
                                            wd_t[:, bf * 2048 + hcg * 128: bf * 2048 + (hcg + 1) * 128],
                                            gatedT[:, hcg * TQ:(hcg + 1) * TQ],
                                            start=(hcg == 0), stop=(hcg == HCG - 1)),
                                  wait=w_, inc="pe" if hcg == HCG - 1 else None)
                        pe_down = sc.cnt["pe"]
                        freed[f"wd{bf}"] = pe_down
                        sc.op("vector", lambda e_, dco=dco, ps=ps, t0=t0:
                              e_.scalar_tensor_tensor(xs_t(dco)[:, t0:t0 + TQ], ps[:], 0.5,
                                                      xs_t(dco)[:, t0:t0 + TQ], ALU.mult, ALU.add),
                              wait=[("pe", pe_down)], inc="dve")
                        freed[f"dps{dco % 2}"] = sc.cnt["dve"]
        sc.barrier()
        sc.flush(block)

    if PHASE_LIMIT >= 1:
        emit_ffn("f1")

    store_qkv_end = 0
    if PHASE_LIMIT >= 2:
        with contextlib.ExitStack() as qst:
            qe = qst.enter_context
            xln2 = qe(nc.sbuf_tensor("xln2", [128, DC * TQ], BF16))
            cos_s = qe(nc.sbuf_tensor("cos_s", [128, T], F32))
            sin_s = qe(nc.sbuf_tensor("sin_s", [128, T], F32))
            qkv_t = qe(nc.sbuf_tensor("qkv_t", [128, 2 * DM], BF16))
            xcast2 = qe(nc.sbuf_tensor("xc2", [128, 2 * TQ], BF16))
            xsq2 = qe(nc.sbuf_tensor("xq2", [128, 2 * TQ], BF16))
            tmp2 = qe(nc.sbuf_tensor("tmp2", [128, TQ], F32))
            qf = qe(nc.sbuf_tensor("qf", [128, TQ], F32))
            qb2 = qe(nc.sbuf_tensor("qb2", [128, TQ], BF16))
            qq2 = qe(nc.sbuf_tensor("qq2", [128, TQ], BF16))
            qn_t = qe(nc.sbuf_tensor("qn_t", [128, TQ], F32))
            qr1 = qe(nc.sbuf_tensor("qr1", [128, TQ], F32))
            qr2 = qe(nc.sbuf_tensor("qr2", [128, TQ], F32))
            qro = qe(nc.sbuf_tensor("qro", [128, TQ], BF16))
            qsw = qe(nc.sbuf_tensor("qsw", [128, TQ], F32))
            vtm = qe(nc.sbuf_tensor("vtm", [128, TQ], BF16))
            vec2 = qe(nc.sbuf_tensor("vec2", [1, 5 * TQ], F32))
            ps_q1 = qe(nc.psum_tensor("ps_q1", [128, TQ], F32))
            ps_q2 = qe(nc.psum_tensor("ps_q2", [128, TQ], F32))
            ps_t1 = qe(nc.psum_tensor("ps_t1", [1, TQ], F32))
            ps_t2 = qe(nc.psum_tensor("ps_t2", [1, TQ], F32))
            ps_b1 = qe(nc.psum_tensor("ps_b1", [128, TQ], F32))
            ps_b2 = qe(nc.psum_tensor("ps_b2", [128, TQ], F32))
            ps_vt = qe(nc.psum_tensor("ps_vt", [128, TQ], BF16))

            vsl2 = tuple(vec2[:, i * TQ:(i + 1) * TQ] for i in range(5))
            V2_MU, V2_M2, V2_VAR, V2_RSTD, V2_NMR = vsl2
            ps_q = [ps_q1, ps_q2]
            freed = {"qkv0": 0, "qkv1": 0, "qev0": 0, "qev1": 0}

            sc.op("sync", lambda e_: e_.dma_start(out=cos_s[:], in_=cos_fm[:, :]), inc16="ld")
            sc.op("sync", lambda e_: e_.dma_start(out=sin_s[:], in_=sin_fm[:, :]), inc16="ld")
            ld_rope = sc.cnt["ld"]

            for q in range(NQ):
                t0 = q * TQ
                dve_xln2 = emit_stream_ln(t0, xln2, xcast2, xsq2, tmp2, vsl2,
                                          ps_b1, ps_b2, ps_t1, ps_t2)
                for oc in range(OC):
                    bf = oc % 2
                    ps = ps_q[oc % 2]
                    sc.op("sync", lambda e_, oc=oc, bf=bf:
                          e_.dma_start(out=qkv_t[:, bf * DM:(bf + 1) * DM], in_=qkvT_d[oc]),
                          wait=[("pe", freed[f"qkv{bf}"])], inc16="wld")
                    wld_oc = sc.cnt["wld"]
                    for dc in range(DC):
                        w_ = None
                        if dc == 0:
                            w_ = [("wld", wld_oc), ("dve", dve_xln2),
                                  ("act", freed[f"qev{oc % 2}"])]
                        sc.op("tensor", lambda e_, dc=dc, bf=bf, ps=ps:
                              e_.matmul(ps[:],
                                        qkv_t[:, bf * DM + dc * 128: bf * DM + (dc + 1) * 128],
                                        xln2[:, dc * TQ:(dc + 1) * TQ],
                                        start=(dc == 0), stop=(dc == DC - 1)),
                              wait=w_, inc="pe" if dc == DC - 1 else None)
                    pe_qkv = sc.cnt["pe"]
                    freed[f"qkv{bf}"] = pe_qkv

                    if oc < NHEAD + NKV:
                        is_q = oc < NHEAD
                        gg, bb = (qng_s, qnb_s) if is_q else (kng_s, knb_s)
                        dve_pre = sc.cnt["dve"]
                        sc.op("scalar", lambda e_, ps=ps: e_.copy(qf[:], ps[:]),
                              wait=[("pe", pe_qkv), ("dve", dve_pre),
                                    ("pool", sc.cnt["pool"])], inc="act")
                        freed[f"qev{oc % 2}"] = sc.cnt["act"]
                        act_e = sc.cnt["act"]
                        sc.op("gpsimd", lambda e_: e_.tensor_copy(qb2[:], qf[:]),
                              wait=[("act", act_e), ("pe", sc.cnt["pe"])], inc="pool")
                        sc.op("scalar", lambda e_: e_.activation(qq2[:], qf[:], AF.Square),
                              inc="act")
                        sc.op("tensor", lambda e_: e_.matmul(ps_t1[:], ones_b[:], qb2[:],
                                                             start=True, stop=True),
                              wait=[("pool", sc.cnt["pool"]), ("dve", sc.cnt["dve"])])
                        sc.op("tensor", lambda e_: e_.matmul(ps_t2[:], ones_b[:], qq2[:],
                                                             start=True, stop=True),
                              wait=[("act", sc.cnt["act"])], inc="pe")
                        pe_st = sc.cnt["pe"]
                        sc.op("vector", lambda e_: e_.tensor_scalar_mul(V2_MU, ps_t1[:], 1.0 / HD),
                              wait=[("pe", pe_st)], inc="dve")
                        sc.op("vector", lambda e_: e_.tensor_scalar_mul(V2_M2, ps_t2[:], 1.0 / HD),
                              inc="dve")
                        sc.op("vector", lambda e_: e_.tensor_mul(V2_VAR, V2_MU, V2_MU), inc="dve")
                        sc.op("vector", lambda e_: e_.tensor_sub(V2_VAR, V2_M2, V2_VAR), inc="dve")
                        sc.op("vector", lambda e_: e_.tensor_scalar_add(V2_VAR, V2_VAR, EPS),
                              inc="dve")
                        sc.op("scalar", lambda e_: e_.activation(V2_RSTD, V2_VAR, AF.Ln),
                              wait=[("dve", sc.cnt["dve"])], inc="act")
                        sc.op("scalar", lambda e_: e_.activation(V2_RSTD, V2_RSTD, AF.Exp, scale=-0.5),
                              inc="act")
                        a_s = sc.cnt["act"]
                        sc.op("vector", lambda e_: e_.tensor_mul(V2_NMR, V2_MU, V2_RSTD),
                              wait=[("act", a_s)], inc="dve")
                        sc.op("vector", lambda e_: e_.tensor_scalar_mul(V2_NMR, V2_NMR, -1.0),
                              inc="dve")
                        d_s = sc.cnt["dve"]
                        sc.op("tensor", lambda e_: e_.matmul(ps_b1[:], ones_f[:], V2_RSTD,
                                                             start=True, stop=True),
                              wait=[("dve", d_s)])
                        sc.op("tensor", lambda e_: e_.matmul(ps_b2[:], ones_f[:], V2_NMR,
                                                             start=True, stop=True), inc="pe")
                        pe_b = sc.cnt["pe"]
                        sc.op("vector", lambda e_: e_.tensor_mul(tmp2[:], qf[:], ps_b1[:]),
                              wait=[("pe", pe_b)], inc="dve")
                        sc.op("vector", lambda e_: e_.tensor_add(tmp2[:], tmp2[:], ps_b2[:]),
                              inc="dve")
                        sc.op("vector", lambda e_, gg=gg, bb=bb:
                              e_.tensor_scalar(qn_t[:], tmp2[:], gg[:], bb[:],
                                               ALU.mult, ALU.add),
                              inc="dve")
                        dve_qn = sc.cnt["dve"]
                        sc.op("scalar", lambda e_: e_.dma_start(out=qsw[0:64, :],
                                                                in_=qn_t[64:128, :]),
                              wait=[("dve", dve_qn)], inc16="swp")
                        sc.op("scalar", lambda e_: e_.dma_start(out=qsw[64:128, :],
                                                                in_=qn_t[0:64, :]),
                              inc16="swp")
                        swp_ro = sc.cnt["swp"]
                        sc.op("vector", lambda e_, t0=t0:
                              e_.tensor_mul(qr1[:], qn_t[:], cos_s[:, t0:t0 + TQ]),
                              wait=[("ld", ld_rope)], inc="dve")
                        sc.op("vector", lambda e_, t0=t0:
                              e_.tensor_mul(qr2[:], qsw[:], sin_s[:, t0:t0 + TQ]),
                              wait=[("swp", swp_ro)], inc="dve")
                        sc.op("vector", lambda e_: e_.tensor_add(qro[:], qr1[:], qr2[:]),
                              wait=[("store", sc.cnt["store"])], inc="dve")
                        dve_ro = sc.cnt["dve"]
                        if is_q:
                            sc.op("scalar", lambda e_, oc=oc, t0=t0:
                                  e_.dma_start(out=q_d[oc, :, t0:t0 + TQ], in_=qro[:]),
                                  wait=[("dve", dve_ro)], inc16="store")
                        else:
                            kv = oc - NHEAD
                            sc.op("scalar", lambda e_, kv=kv, t0=t0:
                                  e_.dma_start(out=kex_in[kv * HD:(kv + 1) * HD, t0:t0 + TQ],
                                               in_=qro[:]),
                                  wait=[("dve", dve_ro)], inc16="store")
                    else:
                        kv = oc - NHEAD - NKV
                        dve_pre = sc.cnt["dve"]
                        sc.op("scalar", lambda e_, ps=ps: e_.copy(vtm[:], ps[:]),
                              wait=[("pe", pe_qkv), ("dve", dve_pre), ("store", sc.cnt["store"])],
                              inc="act")
                        freed[f"qev{oc % 2}"] = sc.cnt["act"]
                        act_v = sc.cnt["act"]
                        for j in range(4):
                            sc.op("tensor", lambda e_, j=j:
                                  e_.transpose(ps_vt[:, j * 128:(j + 1) * 128],
                                               vtm[:, j * 128:(j + 1) * 128], ident_b[:]),
                                  wait=[("act", act_v), ("dve", sc.cnt["dve"])] if j == 0 else None,
                                  inc="pe" if j == 3 else None)
                        pe_vt = sc.cnt["pe"]
                        sc.op("vector", lambda e_: e_.tensor_copy(vtm[:], ps_vt[:]),
                              wait=[("pe", pe_vt), ("act", sc.cnt["act"])], inc="dve")
                        dve_v = sc.cnt["dve"]
                        sc.op("scalar", lambda e_, kv=kv, t0=t0:
                              e_.dma_start(
                                  out=vex_in.rearrange("(kv tc p) e -> kv p tc e", kv=NKV, p=128)
                                      [kv][:, t0 // 128: t0 // 128 + 4, :],
                                  in_=vtm.rearrange("p (tc e) -> p tc e", e=128)),
                              wait=[("dve", dve_v)], inc16="store")
            store_qkv_end = sc.cnt["store"]
            sc.barrier()
            sc.flush(block)

    if PHASE_LIMIT >= 3:
        sc.wait("gpsimd", "store", store_qkv_end)
        sc.op("gpsimd", lambda e_: e_.collective_compute(
            "AllGather", ALU.bypass,
            replica_groups=[[0, 1], [2, 3], [4, 5], [6, 7]],
            ins=[kex_in[:, :]], outs=[kex_out[:, :]]), inc="cc")
        sc.op("gpsimd", lambda e_: e_.collective_compute(
            "AllGather", ALU.bypass,
            replica_groups=[[0, 1], [2, 3], [4, 5], [6, 7]],
            ins=[vex_in[:, :]], outs=[vex_out[:, :]]), inc="cc")
        sc.flush(block)

        with contextlib.ExitStack() as ast_:
            ae = ast_.enter_context
            KVg = ae(nc.sbuf_tensor("KVg", [128, 4 * T], BF16))
            Ssb = ae(nc.sbuf_tensor("Ssb", [128, 2 * T], F32))
            Psb = ae(nc.sbuf_tensor("Psb", [128, 8 * 512], BF16))
            PTs = ae(nc.sbuf_tensor("PTs", [128, 2 * 512], BF16))
            avT = ae(nc.sbuf_tensor("avT", [128, 4 * T], BF16))
            woT_s = ae(nc.sbuf_tensor("woT_s", [128, 4 * DM], BF16))
            Qh = ae(nc.sbuf_tensor("Qh", [128, 2 * 128], BF16))
            avsb = ae(nc.sbuf_tensor("avsb", [128, 2 * 128], BF16))
            mrow = ae(nc.sbuf_tensor("mrow", [128, 64], F32))
            ps_S1 = ae(nc.psum_tensor("ps_S1", [128, 512], F32))
            ps_S2 = ae(nc.psum_tensor("ps_S2", [128, 512], F32))
            ps_PT1 = ae(nc.psum_tensor("ps_PT1", [128, 512], BF16))
            ps_PT2 = ae(nc.psum_tensor("ps_PT2", [128, 512], BF16))
            ps_AV1 = ae(nc.psum_tensor("ps_AV1", [128, 128], F32))
            ps_AV2 = ae(nc.psum_tensor("ps_AV2", [128, 128], F32))
            ps_W1 = ae(nc.psum_tensor("ps_W1", [128, TQ], F32))
            ps_W2 = ae(nc.psum_tensor("ps_W2", [128, TQ], F32))

            Kp = KVg[:, 0:2 * T]
            Vp = KVg[:, 2 * T:4 * T]
            ps_S = [ps_S1, ps_S2]
            ps_PT = [ps_PT1, ps_PT2]
            ps_AVp = [ps_AV1, ps_AV2]
            ps_W = [ps_W1, ps_W2]
            NM = [mrow[:, 0:1], mrow[:, 1:2]]
            DNv = [mrow[:, 2:3], mrow[:, 3:4]]
            RCv = [mrow[:, 4:5], mrow[:, 5:6]]
            SPC = mrow[:, 6:7]
            SPC2 = mrow[:, 7:8]
            DNs = [mrow[:, 32:40], mrow[:, 40:48]]
            mask_t = ae(nc.sbuf_tensor("mask_t", [128, 256], F32))

            sc.op("gpsimd", lambda e_: e_.memset(mrow[:], 0.0), inc="pool")
            sc.op("gpsimd", lambda e_: e_.memset(Ssb[:], 0.0), inc="pool")
            sc.op("gpsimd", lambda e_: e_.memset(Psb[:], 0.0), inc="pool")
            sc.op("gpsimd", lambda e_: e_.iota(mask_t[:], [[1, 256]], base=0,
                                               channel_multiplier=0,
                                               allow_small_or_imprecise_dtypes=True),
                  inc="pool")
            pool0 = sc.cnt["pool"]
            sc.wait("scalar", "pool", pool0)
            sc.op("vector", lambda e_:
                  e_.tensor_scalar(mask_t[:], mask_t[:], qsh_s[:], float(NEG),
                                   ALU.is_gt, ALU.mult),
                  wait=[("pool", pool0)], inc="dve")
            dve_me = sc.cnt["dve"]

            kex_v = kex_out.rearrange("(half kv p) t -> half kv p t", half=2, p=128)
            vex_v = vex_out.rearrange("(half kv tc p) e -> half kv tc p e",
                                      half=2, kv=NKV, p=128)

            units = []
            for g in range(NKV):
                for hh in range(4):
                    for tb in range(NB):
                        units.append((g, hh, tb, 256 * (tb + 1)))
            if ATT_UNITS is not None:
                units = units[:ATT_UNITS]
            NU = len(units)

            pe_S_end = {}
            pe_AV_end = {}
            pe_AT = {}
            pe_PTs = {}
            dve_PTc = {}
            act_exp = {}
            act_dn = {}
            dve_nm = {}
            dve_dn = {}
            dve_scale = {}
            ssb_act = [0] * 8
            fr_pt = [0, 0]
            fr_pts = [0, 0]
            fr_psb = [0] * 8
            fr_av = [0, 0]
            fr_s = [0, 0]
            fr_qh = [0, 0]
            fr_W = [0, 0]
            ld_qh = {}
            ld_kp = {}
            ld_vp = {}
            ld_wo = {}
            dve_avT_g = {}
            wo_block_pe = {}
            s_bank = [0]
            wo_i = [0]
            sec_pe = {}

            def secs(W):
                out = []
                j = 0
                w0 = 0
                while w0 < W:
                    w1 = min(W, w0 + 512)
                    out.append((j, w0, w1))
                    j += 1
                    w0 = w1
                return out

            def emit_kp_loads(g, pe_gate):
                w_l = [("cc", 1), ("store", store_qkv_end)]
                if pe_gate:
                    w_l.append(("pe", pe_gate))
                for j in range(16):
                    sc.op("sync", lambda e_, g=g, j=j: e_.dma_start(
                        out=Kp.rearrange("p (j half k) -> p j half k", half=2, k=128)[:, j],
                        in_=kex_v[:, g].rearrange("half p (j k) -> p j half k", k=128)[:, j],
                        allow_slow_non_contiguous=True),
                        wait=w_l if j == 0 else None, inc16="ld")
                ld_kp[g] = sc.cnt["ld"]

            def emit_vp_loads(g, pe_gate):
                w_l = [("cc", 2), ("store", store_qkv_end)]
                if pe_gate:
                    w_l.append(("pe", pe_gate))
                for tc in range(16):
                    sc.op("sync", lambda e_, g=g, tc=tc: e_.dma_start(
                        out=Vp.rearrange("p (tc half e) -> p tc half e", half=2, e=128)[:, tc],
                        in_=vex_v[:, g].rearrange("half tc p e -> p tc half e")[:, tc],
                        allow_slow_non_contiguous=True),
                        wait=w_l if tc == 0 else None, inc16="ld")
                ld_vp[g] = sc.cnt["ld"]

            def emit_wo_loads(g, pe_gate):
                for j in range(4):
                    w_l = [("pe", pe_gate)] if (j == 0 and pe_gate) else None
                    sc.op("sync", lambda e_, g=g, j=j: e_.dma_start(
                        out=woT_s[:, j * DM:(j + 1) * DM],
                        in_=woT_d[(4 * g + j) * 128:(4 * g + j + 1) * 128, :]),
                        wait=w_l, inc16="ld")
                ld_wo[g] = sc.cnt["ld"]

            def emit_wo_block(gb, c):
                adds = []

                def emit_add(rec):
                    dco, pw, pe_c = rec
                    sc.op("vector", lambda e_, dco=dco, pw=pw, c=c:
                          e_.tensor_add(xs_t(dco)[:, c * TQ:(c + 1) * TQ],
                                        xs_t(dco)[:, c * TQ:(c + 1) * TQ], ps_W[pw][:]),
                          wait=[("pe", pe_c)], inc="dve")
                    fr_W[pw] = sc.cnt["dve"]

                for dco in range(DC):
                    pw = wo_i[0] % 2
                    wo_i[0] += 1
                    for j in range(4):
                        w_l = None
                        if j == 0:
                            w_l = [("dve", fr_W[pw]), ("dve", dve_avT_g[gb]),
                                   ("ld", ld_wo[gb])]
                        sc.op("tensor", lambda e_, pw=pw, j=j, dco=dco, c=c:
                              e_.matmul(ps_W[pw][:],
                                        woT_s[:, j * DM + dco * 128: j * DM + (dco + 1) * 128],
                                        avT[:, j * T + c * TQ: j * T + (c + 1) * TQ],
                                        start=(j == 0), stop=(j == 3)),
                              wait=w_l, inc="pe" if j == 3 else None)
                    adds.append((dco, pw, sc.cnt["pe"]))
                    if len(adds) >= 2:
                        emit_add(adds.pop(0))
                emit_add(adds.pop(0))
                wo_block_pe[(gb, c)] = sc.cnt["pe"]

            emit_wo_loads(0, 0)

            for it in range(NU + 2):
                u = it if it < NU else None
                v = it - 1 if 0 <= it - 1 < NU else None
                w = it - 2 if 0 <= it - 2 < NU else None

                if u is not None:
                    g, hh, tb, W = units[u]
                    h = 4 * g + hh
                    if hh == 0 and tb == 0:
                        gate = pe_S_end[u - 1] if u > 0 else 0
                        emit_kp_loads(g, gate)
                    sc.op("sync", lambda e_, h=h, tb=tb, u=u: e_.dma_start(
                        out=Qh[:, (u % 2) * 128:(u % 2) * 128 + 128],
                        in_=q_d[h][:, tb * 128:(tb + 1) * 128]),
                        wait=[("pe", fr_qh[u % 2]), ("store", store_qkv_end)],
                        inc16="ld")
                    ld_qh[u] = sc.cnt["ld"]
                    if ATT_WO and hh == 0 and tb in (2, 6, 10, 14) and g > 0:
                        emit_wo_block(g - 1, (tb - 2) // 4)
                        if tb == 14:
                            emit_wo_loads(g, wo_block_pe[(g - 1, 3)])

                if u is not None:
                    g, hh, tb, W = units[u]
                    h = 4 * g + hh
                    slist = secs(W)
                    nblk = len(slist)
                    for (j, w0, w1) in slist:
                        sb = s_bank[0] % 2
                        s_bank[0] += 1
                        ln = w1 - w0
                        w_l = [("dve", fr_s[sb])]
                        if j == 0:
                            w_l += [("ld", ld_qh[u]), ("ld", ld_kp[g])]
                        sc.op("tensor", lambda e_, sb=sb, w0=w0, w1=w1, u=u:
                              e_.matmul(ps_S[sb][:, 0:w1 - w0],
                                        Qh[:, (u % 2) * 128:(u % 2) * 128 + 128],
                                        Kp[:, w0:w1], start=True, stop=True),
                              wait=w_l, inc="pe")
                        sec_pe[(u, j)] = sc.cnt["pe"]
                        ev_l = [("pe", sec_pe[(u, j)]), ("act", ssb_act[j])]
                        if u == 0 and j == 0:
                            ev_l.append(("dve", dve_me))
                        if j < nblk - 1:
                            sc.op("vector", lambda e_, sb=sb, w0=w0, w1=w1:
                                  e_.tensor_copy(Ssb[:, w0:w1], ps_S[sb][:, 0:w1 - w0]),
                                  wait=ev_l, inc="dve")
                        else:
                            if ln == 512:
                                sc.op("vector", lambda e_, sb=sb, w0=w0:
                                      e_.tensor_copy(Ssb[:, w0:w0 + 256],
                                                     ps_S[sb][:, 0:256]),
                                      wait=ev_l, inc="dve")
                                ev_l = None
                            sc.op("vector", lambda e_, sb=sb, w1=w1, ln=ln:
                                  e_.tensor_add(Ssb[:, w1 - 256:w1],
                                                ps_S[sb][:, ln - 256:ln], mask_t[:]),
                                  wait=ev_l, inc="dve")
                        fr_s[sb] = sc.cnt["dve"]
                    pe_S_end[u] = sc.cnt["pe"]
                    fr_qh[u % 2] = sc.cnt["pe"]
                    sc.op("vector", lambda e_: e_.tensor_scalar_add(SPC, qsh_s[:], 0.0),
                          inc="dve")
                    sc.op("vector", lambda e_, u=u, W=W:
                          e_.reduce_max(NM[u % 2], Ssb[:, 0:W],
                                        axis=AX.X, negate=True), inc="dve")
                    sc.op("vector", lambda e_: e_.tensor_scalar_add(SPC, qsh_s[:], 0.0),
                          inc="dve")
                    dve_nm[u] = sc.cnt["dve"]

                if w is not None:
                    gw, hw, tw, Ww = units[w]
                    b_at = len(secs(Ww)) % 2
                    sc.op("tensor", lambda e_, b_at=b_at, w=w:
                          e_.transpose(ps_PT[b_at][:, 0:128],
                                       avsb[:, (w % 2) * 128:(w % 2) * 128 + 128],
                                       ident_b[:]),
                          wait=[("dve", dve_scale[w]), ("dve", fr_pt[b_at])], inc="pe")
                    pe_AT[w] = sc.cnt["pe"]

                if w is not None:
                    gw, hw, tw, Ww = units[w]
                    b_at = len(secs(Ww)) % 2
                    w_l = [("pe", pe_AT[w])]
                    if ATT_WO and hw == 0 and gw > 0:
                        w_l.append(("pe", wo_block_pe[(gw - 1, tw // 4)]))
                    sc.op("vector", lambda e_, hw=hw, tw=tw, b_at=b_at:
                          e_.tensor_copy(avT[:, hw * T + tw * 128: hw * T + (tw + 1) * 128],
                                         ps_PT[b_at][:, 0:128]),
                          wait=w_l, inc="dve")
                    fr_pt[b_at] = sc.cnt["dve"]
                    dve_avT_g[gw] = sc.cnt["dve"]

                if v is not None:
                    gv, hv, tv, Wv = units[v]
                    vlist = secs(Wv)
                    nkcW = Wv // 128
                    for (j, w0, w1) in vlist:
                        bb = j % 2
                        ln = w1 - w0
                        nkc = ln // 128
                        for i_ in range(nkc):
                            w_l = None
                            if i_ == 0:
                                w_l = [("act", act_exp[(v, j)]), ("dve", fr_pt[bb])]
                            sc.op("tensor", lambda e_, bb=bb, i_=i_, j=j:
                                  e_.transpose(ps_PT[bb][:, i_ * 128:(i_ + 1) * 128],
                                               Psb[:, j * 512 + i_ * 128: j * 512 + (i_ + 1) * 128],
                                               ident_b[:]),
                                  wait=w_l, inc="pe" if i_ == nkc - 1 else None)
                        pe_PTs[(v, j)] = sc.cnt["pe"]
                        fr_psb[j] = sc.cnt["pe"]
                        sc.op("vector", lambda e_, bb=bb, ln=ln:
                              e_.tensor_copy(PTs[:, bb * 512: bb * 512 + ln],
                                             ps_PT[bb][:, 0:ln]),
                              wait=[("pe", pe_PTs[(v, j)]), ("pe", fr_pts[bb])],
                              inc="dve")
                        dve_PTc[(v, j)] = sc.cnt["dve"]
                        fr_pt[bb] = sc.cnt["dve"]
                        for i_ in range(nkc):
                            kc = 4 * j + i_
                            w_l = None
                            if i_ == 0:
                                w_l = [("dve", dve_PTc[(v, j)])]
                                if kc == 0:
                                    w_l += [("dve", fr_av[v % 2]),
                                            ("ld", ld_vp[gv])]
                            sc.op("tensor", lambda e_, bb=bb, i_=i_, kc=kc, v=v,
                                  first=(kc == 0), last=(kc == nkcW - 1):
                                  e_.matmul(ps_AVp[v % 2][:],
                                            PTs[:, bb * 512 + i_ * 128: bb * 512 + (i_ + 1) * 128],
                                            Vp[:, kc * 128:(kc + 1) * 128],
                                            start=first, stop=last, skip_group_check=True),
                                  wait=w_l, inc="pe" if i_ == nkc - 1 else None)
                        fr_pts[bb] = sc.cnt["pe"]
                    pe_AV_end[v] = sc.cnt["pe"]

                if u is not None:
                    g, hh, tb, W = units[u]
                    slist = secs(W)
                    for (j, w0, w1) in slist:
                        w_l = [("pe", fr_psb[j])]
                        if j == 0:
                            w_l.append(("dve", dve_nm[u]))
                            if u >= 2:
                                w_l.append(("dve", dve_dn[u - 2]))
                        sc.op("scalar", lambda e_, w0=w0, w1=w1, u=u, j=j:
                              e_.activation(Psb[:, j * 512: j * 512 + (w1 - w0)],
                                            Ssb[:, w0:w1], AF.Exp, bias=NM[u % 2],
                                            accum_out=DNs[u % 2][:, j:j + 1]),
                              wait=w_l, inc="act")
                        act_exp[(u, j)] = sc.cnt["act"]
                        ssb_act[j] = sc.cnt["act"]
                    sc.op("scalar", lambda e_: e_.copy(SPC2, qsh_s[:]), inc="act")
                    act_dn[u] = sc.cnt["act"]

                if v is not None:
                    gv, hv, tv, Wv = units[v]
                    nblk = len(secs(Wv))
                    sc.op("vector", lambda e_, v=v, nblk=nblk:
                          e_.reduce_sum(DNv[v % 2], DNs[v % 2][:, 0:nblk], axis=AX.X),
                          wait=[("act", act_dn[v])], inc="dve")
                    sc.op("vector", lambda e_: e_.tensor_scalar_add(SPC, qsh_s[:], 0.0),
                          inc="dve")
                    dve_dn[v] = sc.cnt["dve"]
                    sc.op("vector", lambda e_, v=v: e_.reciprocal(RCv[v % 2], DNv[v % 2]),
                          inc="dve")
                    sc.op("vector", lambda e_: e_.tensor_scalar_add(SPC, qsh_s[:], 0.0),
                          inc="dve")
                    sc.op("vector", lambda e_, v=v:
                          e_.tensor_scalar_mul(avsb[:, (v % 2) * 128:(v % 2) * 128 + 128],
                                               ps_AVp[v % 2][:], RCv[v % 2]),
                          wait=[("pe", pe_AV_end[v])], inc="dve")
                    dve_scale[v] = sc.cnt["dve"]
                    fr_av[v % 2] = sc.cnt["dve"]
                if v is not None and units[v][1] == 3 and units[v][2] == NB - 1:
                    gnext = units[v][0] + 1
                    if gnext < NKV:
                        emit_vp_loads(gnext, pe_AV_end[v])
                if it == 0:
                    emit_vp_loads(0, 0)

            if ATT_WO and ATT_UNITS is None:
                for c in range(4):
                    emit_wo_block(NKV - 1, c)
            sc.barrier()
            sc.flush(block)

    if PHASE_LIMIT >= 4:
        emit_ffn("f3")

    for dc in range(DC):
        w_ = None
        if dc == 0:
            w_ = [("dve", sc.cnt["dve"]), ("pe", sc.cnt["pe"]), ("act", sc.cnt["act"])]
        sc.op("sync", lambda e_, dc=dc: e_.dma_start(
            out=out_fm[dc * 128:(dc + 1) * 128, :], in_=xs_t(dc)),
            wait=w_, inc16="store")
    sc.wait("sync", "store", sc.cnt["store"])
    sc.flush(block)

    st.close()
    sc.stack.close()
    return nc


def _core_positions(s):
    blocks = [(2 * j + s) for j in range(NB)]
    return np.concatenate([np.arange(b * 128, (b + 1) * 128) for b in blocks])


def _rope_tables(pos):
    inv_freq = (1.0 / (ROPE_BASE ** (np.arange(0, HD, 2, dtype=np.float32) / HD))).astype(np.float32)
    freqs = np.outer(pos.astype(np.float32), inv_freq).astype(np.float32)
    cos = np.concatenate([np.cos(freqs), np.cos(freqs)], axis=-1)
    sin = np.concatenate([-np.sin(freqs), np.sin(freqs)], axis=-1)
    return (np.ascontiguousarray(cos.T.astype(np.float32)),
            np.ascontiguousarray(sin.T.astype(np.float32)))


def kernel(L0_attn_x, L0_attn_w_qkv, L0_attn_w_o, L0_attn_qn_g, L0_attn_qn_b,
           L0_attn_kn_g, L0_attn_kn_b, L0_ffn_w_gate, L0_ffn_w_up, L0_ffn_w_down):
    x = np.asarray(L0_attn_x, dtype=np.float32)
    ident = np.eye(128, dtype=np.float32)
    in_common = {
        "w_qkv": np.ascontiguousarray(np.asarray(L0_attn_w_qkv, np.float32)),
        "w_o": np.ascontiguousarray(np.asarray(L0_attn_w_o, np.float32)),
        "qn_g": np.asarray(L0_attn_qn_g, np.float32).reshape(HD, 1),
        "qn_b": np.asarray(L0_attn_qn_b, np.float32).reshape(HD, 1),
        "kn_g": np.asarray(L0_attn_kn_g, np.float32).reshape(HD, 1),
        "kn_b": np.asarray(L0_attn_kn_b, np.float32).reshape(HD, 1),
        "w_gate": np.ascontiguousarray(np.asarray(L0_ffn_w_gate, np.float32)),
        "w_up": np.ascontiguousarray(np.asarray(L0_ffn_w_up, np.float32)),
        "w_down": np.ascontiguousarray(np.asarray(L0_ffn_w_down, np.float32)),
        "ident_bf_i": ident.astype(ml_dtypes.bfloat16),
        "ident_f_i": ident,
        "kpos": np.ascontiguousarray(np.tile(np.arange(256, dtype=np.float32), (128, 1))),
    }
    in_maps = []
    poss = []
    for c in range(8):
        b, s = c // 2, c % 2
        pos = _core_positions(s)
        poss.append(pos)
        cos_t, sin_t = _rope_tables(pos)
        m = dict(in_common)
        m["x_fm"] = np.ascontiguousarray(x[b][pos].T)
        m["cos_fm"] = cos_t
        m["sin_fm"] = sin_t
        m["qpos"] = np.ascontiguousarray(pos.astype(np.float32).reshape(NB, 128).T)
        m["qsh"] = np.ascontiguousarray((128.0 * s + np.arange(128, dtype=np.float32)).reshape(128, 1))
        in_maps.append(m)

    nc = build_program()
    res = run_bass_kernel_spmd(nc, in_maps, core_ids=list(range(8)))

    out = np.empty((B, S, DM), dtype=np.float32)
    for c in range(8):
        b = c // 2
        out[b][poss[c]] = np.asarray(res.results[c]["out_fm"]).T
    return out


# version 07
# speedup vs baseline: 1.2196x; 1.0183x over previous
import math
import contextlib
import numpy as np
import ml_dtypes

import concourse.bass as bass
import concourse.mybir as mybir
from concourse.bass_utils import run_bass_kernel_spmd

F32 = mybir.dt.float32
BF16 = mybir.dt.bfloat16
AF = mybir.ActivationFunctionType
ALU = mybir.AluOpType
AX = mybir.AxisListType

B, S, DM = 4, 4096, 2048
NHEAD, NKV, HD = 16, 4, 128
DH = 8192
EPS = 1e-5
ROPE_BASE = 1e6

T = 2048
DC = DM // 128
NQ = 4
TQ = T // NQ
HC = DH // 128
G = 4
HCG = HC // G
OC = (NHEAD + 2 * NKV) * HD // 128
NB = T // 128
SCL = 1.0 / math.sqrt(HD)
NEG = -1e9

PHASE_LIMIT = 99
ATT_WO = True
ATT_UNITS = None
DEBUG_FFN = False
DEBUG_ATT = False


class Sched:
    ENGINES = ("sync", "tensor", "vector", "scalar", "gpsimd")

    def __init__(self, nc):
        self.nc = nc
        self.q = {e: [] for e in self.ENGINES}
        self.sem = {}
        self.cnt = {}
        self.stack = contextlib.ExitStack()

    def new_sem(self, name):
        self.sem[name] = self.stack.enter_context(self.nc.semaphore(name))
        self.cnt[name] = 0

    def op(self, eng, f, wait=None, inc=None, inc16=None):
        waits = [(self.sem[s], v) for (s, v) in (wait or []) if v > 0]
        incs = []
        for s in ([inc] if isinstance(inc, str) else (inc or [])):
            self.cnt[s] += 1
            incs.append((self.sem[s], 1))
        for s in ([inc16] if isinstance(inc16, str) else (inc16 or [])):
            self.cnt[s] += 16
            incs.append((self.sem[s], 16))

        def emit(e, waits=waits, incs=incs, f=f):
            for (h, v) in waits:
                e.wait_ge(h, v)
            ins = f(e)
            for (h, k) in incs:
                ins.then_inc(h, k)
        self.q[eng].append(emit)

    def wait(self, eng, s, v=None):
        v = self.cnt[s] if v is None else v
        h = self.sem[s]
        if v > 0:
            self.q[eng].append(lambda e, h=h, v=v: e.wait_ge(h, v))

    def barrier(self):
        if "bar" not in self.sem:
            self.new_sem("bar")
        tgt = self.cnt["bar"] + len(self.ENGINES)
        self.cnt["bar"] = tgt
        h = self.sem["bar"]
        for eng in self.ENGINES:
            def emit(e, h=h, tgt=tgt):
                e.nop().then_inc(h, 1)
                e.wait_ge(h, tgt)
            self.q[eng].append(emit)

    def flush(self, block):
        for eng in self.ENGINES:
            ops = self.q[eng]
            if not ops:
                continue
            self.q[eng] = []

            def run(e, ops=ops):
                for f in ops:
                    f(e)
            getattr(block, eng)(run)


def build_program():
    nc = bass.Bass("TRN2", target_bir_lowering=False)

    def din(name, shape, dt=F32):
        return nc.dram_tensor(name, shape, dt, kind="ExternalInput").ap()

    x_fm = din("x_fm", [DM, T])
    w_qkv = din("w_qkv", [(NHEAD + 2 * NKV) * HD, DM])
    w_o = din("w_o", [DM, NHEAD * HD])
    qn_g, qn_b = din("qn_g", [HD, 1]), din("qn_b", [HD, 1])
    kn_g, kn_b = din("kn_g", [HD, 1]), din("kn_b", [HD, 1])
    w_gate = din("w_gate", [DH, DM])
    w_up = din("w_up", [DH, DM])
    w_down = din("w_down", [DM, DH])
    cos_fm = din("cos_fm", [HD, T])
    sin_fm = din("sin_fm", [HD, T])
    qpos_in = din("qpos", [128, NB])
    kpos_in = din("kpos", [128, 256])
    qsh_in = din("qsh", [128, 1])
    ident_bf_in = din("ident_bf_i", [128, 128], BF16)
    ident_f_in = din("ident_f_i", [128, 128], F32)

    out_fm = nc.dram_tensor("out_fm", [DM, T], F32, kind="ExternalOutput").ap()
    if DEBUG_FFN:
        dbg_xln = nc.dram_tensor("dbg_xln", [128, DC * TQ], BF16, kind="ExternalOutput").ap()
        dbg_gated = nc.dram_tensor("dbg_gated", [128, HCG * TQ], BF16, kind="ExternalOutput").ap()
    else:
        dbg_xln = dbg_gated = None
    if DEBUG_ATT:
        dbg_S = nc.dram_tensor("dbg_S", [128, 512], F32, kind="ExternalOutput").ap()
        dbg_P = nc.dram_tensor("dbg_P", [128, 512], BF16, kind="ExternalOutput").ap()
        dbg_m = nc.dram_tensor("dbg_m", [128, 24], F32, kind="ExternalOutput").ap()
        dbg_av = nc.dram_tensor("dbg_av", [128, 128], BF16, kind="ExternalOutput").ap()
        dbg_K = nc.dram_tensor("dbg_K", [128, 512], BF16, kind="ExternalOutput").ap()
        dbg_Q = nc.dram_tensor("dbg_Q", [128, 256], BF16, kind="ExternalOutput").ap()
    else:
        dbg_S = dbg_P = dbg_m = dbg_av = dbg_K = dbg_Q = None

    def dint(name, shape, dt=BF16):
        return nc.dram_tensor(name, shape, dt, kind="Internal").ap()

    wgT_d = dint("wgT_d", [HC, 128, DM])
    wuT_d = dint("wuT_d", [HC, 128, DM])
    wdT_d = dint("wdT_d", [DC, G, 128, 2048])
    qkvT_d = dint("qkvT_d", [OC, 128, DM])
    woT_d = dint("woT_d", [NHEAD * HD, DM])
    q_d = dint("q_d", [NHEAD, HD, T])
    kex_in = dint("kex_in", [NKV * HD, T])
    kex_out = dint("kex_out", [2 * NKV * HD, T])
    vex_in = dint("vex_in", [NKV * T, HD])
    vex_out = dint("vex_out", [2 * NKV * T, HD])

    st = contextlib.ExitStack()
    e = st.enter_context

    xs = e(nc.sbuf_tensor("xs", [128, DC * T], F32))
    ident_b = e(nc.sbuf_tensor("ident_b", [128, 128], BF16))
    ident_f = e(nc.sbuf_tensor("ident_f", [128, 128], F32))
    ones_b = e(nc.sbuf_tensor("ones_b", [128, 1], BF16))
    ones_f = e(nc.sbuf_tensor("ones_f", [1, 128], F32))
    qpos_s = e(nc.sbuf_tensor("qpos_s", [128, NB], F32))
    qsh_s = e(nc.sbuf_tensor("qsh_s", [128, 1], F32))
    qng_s = e(nc.sbuf_tensor("qng_s", [128, 1], F32))
    qnb_s = e(nc.sbuf_tensor("qnb_s", [128, 1], F32))
    kng_s = e(nc.sbuf_tensor("kng_s", [128, 1], F32))
    knb_s = e(nc.sbuf_tensor("knb_s", [128, 1], F32))

    sc = Sched(nc)
    for s_ in ("ld", "wld", "store", "pe", "act", "dve", "pool", "cc", "swp", "boot"):
        sc.new_sem(s_)

    block = e(nc.Block())

    def xs_t(dc):
        return xs[:, dc * T:(dc + 1) * T]

    BOOT = 1 << 24
    all_sems = [sc.sem[s_] for s_ in ("ld", "wld", "store", "pe", "act", "dve",
                                      "pool", "cc", "swp")]

    sc.new_sem("bar")

    def _boot_clear(e_):
        for h in all_sems:
            e_.sem_clear(h)
        e_.sem_clear(sc.sem["bar"])
        e_.sem_clear(sc.sem["boot"])
        return e_.sem_inc(sc.sem["boot"], BOOT)
    sc.q["sync"].append(lambda e_: _boot_clear(e_))
    for eng in ("tensor", "vector", "scalar", "gpsimd"):
        sc.q[eng].append(lambda e_: e_.wait_ge(sc.sem["boot"], BOOT))

    for dc in range(DC):
        sc.op("sync", lambda e_, dc=dc: e_.dma_start(
            out=xs_t(dc), in_=x_fm[dc * 128:(dc + 1) * 128, :]), inc16="ld")
    for dst, src in ((ident_b, ident_bf_in), (ident_f, ident_f_in), (qpos_s, qpos_in),
                     (qsh_s, qsh_in),
                     (qng_s, qn_g), (qnb_s, qn_b), (kng_s, kn_g), (knb_s, kn_b)):
        sc.op("sync", lambda e_, d=dst, s_=src: e_.dma_start(out=d[:], in_=s_[:, :]), inc16="ld")
    ld_p0 = sc.cnt["ld"]
    sc.op("gpsimd", lambda e_: e_.memset(ones_b[:], 1.0), inc="pool")
    sc.op("gpsimd", lambda e_: e_.memset(ones_f[:], 1.0), inc="pool")
    sc.op("vector", lambda e_: e_.tensor_scalar_mul(qng_s[:], qng_s[:], SCL),
          wait=[("ld", ld_p0)], inc="dve")
    sc.op("vector", lambda e_: e_.tensor_scalar_mul(qnb_s[:], qnb_s[:], SCL), inc="dve")

    with nc.sbuf_tensor("wrm", [128, 128], F32) as wrm, \
         nc.sbuf_tensor("wrmb", [128, 128], BF16) as wrmb, \
         nc.psum_tensor("ps_wrm", [128, 128], F32) as ps_wrm:
        sc.op("gpsimd", lambda e_: e_.iota(wrm[:, 0:128], [[1, 128]], base=0,
                                           channel_multiplier=0,
                                           allow_small_or_imprecise_dtypes=True),
              inc="pool")
        sc.op("gpsimd", lambda e_: e_.tensor_copy(wrmb[:], wrm[:]),
              wait=[("pool", sc.cnt["pool"])], inc="pool")
        sc.op("gpsimd", lambda e_: e_.memset(wrm[:], 1.0),
              wait=[("pool", sc.cnt["pool"])], inc="pool")
        p_w = sc.cnt["pool"]
        for fn in (AF.Exp, AF.Silu, AF.Ln, AF.Square, AF.Copy):
            sc.op("scalar", lambda e_, fn=fn: e_.activation(wrm[:], wrm[:], fn),
                  wait=[("pool", p_w)], inc="act")
        sc.op("vector", lambda e_: e_.reduce_max(wrm[:, 0:1], wrm[:], axis=AX.X,
                                                 negate=True),
              wait=[("act", sc.cnt["act"])], inc="dve")
        sc.op("vector", lambda e_: e_.reciprocal(wrm[:, 1:2], wrm[:, 0:1]), inc="dve")
        sc.op("tensor", lambda e_: e_.matmul(ps_wrm[:], wrmb[:], wrmb[:],
                                             start=True, stop=True),
              wait=[("pool", p_w), ("dve", sc.cnt["dve"])], inc="pe")
        sc.barrier()
        sc.flush(block)

    with nc.sbuf_tensor("prep_nat", [128, DH], F32) as prep_nat, \
         nc.sbuf_tensor("prep_stg", [128, 2048], BF16) as prep_stg, \
         nc.sbuf_tensor("prep_stg2", [128, 2048], BF16) as prep_stg2, \
         nc.psum_tensor("ps_pr1", [128, 512], F32) as ps_pr1, \
         nc.psum_tensor("ps_pr2", [128, 512], F32) as ps_pr2:

        pspr = [ps_pr1, ps_pr2]
        stgs = [prep_stg, prep_stg2]
        freed = {"nat": 0, "stg0": 0, "stg1": 0, "ps0": 0, "ps1": 0}

        def prep_weight(w_ap, O, I, store_fn, nat_cols):
            n_span = max(1, I // 2048)
            span_w = min(I, 2048)
            for r in range(O // 128):
                for piece in range(I // nat_cols):
                    c0 = piece * nat_cols
                    sc.op("sync", lambda e_, r=r, c0=c0, w=nat_cols:
                          e_.dma_start(out=prep_nat[:, c0:c0 + w],
                                       in_=w_ap[r * 128:(r + 1) * 128, c0:c0 + w]),
                          wait=[("pe", freed["nat"])] if piece == 0 else None,
                          inc16="wld")
                wld_here = sc.cnt["wld"]
                for sp in range(n_span):
                    sidx = (r * n_span + sp) % 2
                    stg = stgs[sidx]
                    for quad in range(span_w // 512):
                        pidx = quad % 2
                        ps = pspr[pidx]
                        for j in range(4):
                            ic = quad * 4 + j
                            col = sp * 2048 + ic * 128
                            w_ = []
                            if quad == 0 and j == 0 and sp == 0:
                                w_.append(("wld", wld_here))
                            if j == 0:
                                w_.append(("act", freed[f"ps{pidx}"]))
                            sc.op("tensor", lambda e_, ps=ps, j=j, col=col:
                                  e_.transpose(ps[:, j * 128:(j + 1) * 128],
                                               prep_nat[:, col:col + 128], ident_f[:]),
                                  wait=w_, inc="pe" if j == 3 else None)
                        pe_q = sc.cnt["pe"]
                        sc.op("scalar", lambda e_, ps=ps, stg=stg, quad=quad:
                              e_.copy(stg[:, quad * 512:(quad + 1) * 512], ps[:]),
                              wait=[("pe", pe_q), ("store", freed[f"stg{sidx}"])] if quad == 0
                              else [("pe", pe_q)],
                              inc="act")
                        freed[f"ps{pidx}"] = sc.cnt["act"]
                    store_fn(r, sp, stg)
                    freed[f"stg{sidx}"] = sc.cnt["store"]
                freed["nat"] = sc.cnt["pe"]

        def mk_simple_store(dst):
            def store(r, sp, stg):
                sc.op("sync", lambda e_, r=r, stg=stg:
                      e_.dma_start(out=dst[r], in_=stg[:, :]),
                      wait=[("act", sc.cnt["act"])], inc16="store")
            return store

        def wd_store(r, sp, stg):
            sc.op("sync", lambda e_, r=r, sp=sp, stg=stg:
                  e_.dma_start(out=wdT_d[r, sp], in_=stg[:, :]),
                  wait=[("act", sc.cnt["act"])], inc16="store")

        wo_view = woT_d.rearrange("(ec p) c -> ec p c", p=128)

        def wo_store(r, sp, stg):
            a_ = sc.cnt["act"]
            for ec in range(16):
                sc.op("scalar", lambda e_, r=r, ec=ec, stg=stg:
                      e_.dma_start(out=wo_view[ec, :, r * 128:(r + 1) * 128],
                                   in_=stg[:, ec * 128:(ec + 1) * 128]),
                      wait=[("act", a_)] if ec == 0 else None, inc16="store")

        if PHASE_LIMIT >= 1:
            prep_weight(w_gate, DH, DM, mk_simple_store(wgT_d), 2048)
            prep_weight(w_up, DH, DM, mk_simple_store(wuT_d), 2048)
            prep_weight(w_down, DM, DH, wd_store, 4096)
        if PHASE_LIMIT >= 2:
            prep_weight(w_qkv, OC * 128, DM, mk_simple_store(qkvT_d), 2048)
            prep_weight(w_o, DM, NHEAD * HD, wo_store, 2048)

        sc.barrier()
        sc.flush(block)

    def emit_stream_ln(t0, xln_dst, xcast, xsq, tmp, vslices, ps_a, ps_b, ps_s1, ps_s2):
        V_MU, V_M2, V_VAR, V_RSTD, V_NMR = vslices
        pe_after = {}
        for dc in range(DC):
            sl = (dc % 2) * TQ
            xc = xcast[:, sl:sl + TQ]
            xq_ = xsq[:, sl:sl + TQ]
            w0 = [("pe", sc.cnt["pe"]), ("dve", sc.cnt["dve"])] if dc == 0                 else ([("pe", pe_after[dc - 2])] if dc >= 2 else None)
            sc.op("gpsimd", lambda e_, dc=dc, xc=xc: e_.tensor_copy(xc, xs_t(dc)[:, t0:t0 + TQ]),
                  wait=w0, inc="pool")
            wsq = [("pe", sc.cnt["pe"])] if dc == 0                 else ([("pe", pe_after[dc - 2])] if dc >= 2 else None)
            sc.op("scalar", lambda e_, dc=dc, xq_=xq_: e_.activation(xq_, xs_t(dc)[:, t0:t0 + TQ],
                                                                     AF.Square),
                  wait=wsq, inc="act")
            sc.op("tensor", lambda e_, dc=dc, xc=xc: e_.matmul(ps_s1[:], ones_b[:], xc,
                                                               start=(dc == 0), stop=(dc == DC - 1)),
                  wait=[("pool", sc.cnt["pool"])] + ([("dve", sc.cnt["dve"])] if dc == 0 else []))
            sc.op("tensor", lambda e_, dc=dc, xq_=xq_: e_.matmul(ps_s2[:], ones_b[:], xq_,
                                                                 start=(dc == 0), stop=(dc == DC - 1)),
                  wait=[("act", sc.cnt["act"])], inc="pe")
            pe_after[dc] = sc.cnt["pe"]
        pe_stats = sc.cnt["pe"]
        sc.op("vector", lambda e_: e_.tensor_scalar_mul(V_MU, ps_s1[:], 1.0 / DM),
              wait=[("pe", pe_stats)], inc="dve")
        sc.op("vector", lambda e_: e_.tensor_scalar_mul(V_M2, ps_s2[:], 1.0 / DM), inc="dve")
        sc.op("vector", lambda e_: e_.tensor_mul(V_VAR, V_MU, V_MU), inc="dve")
        sc.op("vector", lambda e_: e_.tensor_sub(V_VAR, V_M2, V_VAR), inc="dve")
        sc.op("vector", lambda e_: e_.tensor_scalar_add(V_VAR, V_VAR, EPS), inc="dve")
        sc.op("scalar", lambda e_: e_.activation(V_RSTD, V_VAR, AF.Ln),
              wait=[("dve", sc.cnt["dve"])], inc="act")
        sc.op("scalar", lambda e_: e_.activation(V_RSTD, V_RSTD, AF.Exp, scale=-0.5), inc="act")
        a_st = sc.cnt["act"]
        sc.op("vector", lambda e_: e_.tensor_mul(V_NMR, V_MU, V_RSTD),
              wait=[("act", a_st)], inc="dve")
        sc.op("vector", lambda e_: e_.tensor_scalar_mul(V_NMR, V_NMR, -1.0), inc="dve")
        d_st = sc.cnt["dve"]
        sc.op("tensor", lambda e_: e_.matmul(ps_a[:], ones_f[:], V_RSTD, start=True, stop=True),
              wait=[("dve", d_st), ("act", sc.cnt["act"])])
        sc.op("tensor", lambda e_: e_.matmul(ps_b[:], ones_f[:], V_NMR, start=True, stop=True),
              inc="pe")
        pe_bc = sc.cnt["pe"]
        for dc in range(DC):
            sc.op("vector", lambda e_, dc=dc: e_.tensor_mul(tmp[:], xs_t(dc)[:, t0:t0 + TQ], ps_a[:]),
                  wait=[("pe", pe_bc)] if dc == 0 else None, inc="dve")
            sc.op("vector", lambda e_, dc=dc: e_.tensor_add(xln_dst[:, dc * TQ:(dc + 1) * TQ],
                                                            tmp[:], ps_b[:]), inc="dve")
        return sc.cnt["dve"]

    def emit_ffn(tag):
        with contextlib.ExitStack() as fst:
            fe = fst.enter_context
            xln = fe(nc.sbuf_tensor(f"xln{tag}", [128, DC * TQ], BF16))
            gatedT = fe(nc.sbuf_tensor(f"gatedT{tag}", [128, HCG * TQ], BF16))
            wg_t = fe(nc.sbuf_tensor(f"wg_t{tag}", [128, 2 * DM], BF16))
            wu_t = fe(nc.sbuf_tensor(f"wu_t{tag}", [128, 2 * DM], BF16))
            wd_t = fe(nc.sbuf_tensor(f"wd_t{tag}", [128, 2 * 2048], BF16))
            xcast = fe(nc.sbuf_tensor(f"xc{tag}", [128, 2 * TQ], BF16))
            xsq = fe(nc.sbuf_tensor(f"xq{tag}", [128, 2 * TQ], BF16))
            tmp = fe(nc.sbuf_tensor(f"tmp{tag}", [128, TQ], F32))
            gate_s = fe(nc.sbuf_tensor(f"gs{tag}", [128, TQ], F32))
            up_s = fe(nc.sbuf_tensor(f"us{tag}", [128, TQ], F32))
            vec = fe(nc.sbuf_tensor(f"vec{tag}", [1, 5 * TQ], F32))
            ps_s1 = fe(nc.psum_tensor(f"ps_s1{tag}", [1, TQ], F32))
            ps_s2 = fe(nc.psum_tensor(f"ps_s2{tag}", [1, TQ], F32))
            ps_g = fe(nc.psum_tensor(f"ps_g{tag}", [128, TQ], F32))
            ps_u = fe(nc.psum_tensor(f"ps_u{tag}", [128, TQ], F32))
            ps_d1 = fe(nc.psum_tensor(f"ps_d1{tag}", [128, TQ], F32))
            ps_d2 = fe(nc.psum_tensor(f"ps_d2{tag}", [128, TQ], F32))

            ps_d = [ps_d1, ps_d2]
            vsl = tuple(vec[:, i * TQ:(i + 1) * TQ] for i in range(5))
            freed = {"wg0": 0, "wg1": 0, "wu0": 0, "wu1": 0, "wd0": 0, "wd1": 0,
                     "gate_ev": 0, "up_ev": 0, "dps0": 0, "dps1": 0}

            for q in range(NQ):
                t0 = q * TQ
                dve_xln = emit_stream_ln(t0, xln, xcast, xsq, tmp, vsl,
                                         ps_g, ps_u, ps_s1, ps_s2)
                if DEBUG_FFN and tag == "f1" and q == 0:
                    sc.op("scalar", lambda e_: e_.dma_start(out=dbg_xln[:, :], in_=xln[:]),
                          wait=[("dve", dve_xln)], inc16="store")
                for g in range(G):
                    for hcg in range(HCG):
                        hc = g * HCG + hcg
                        bf = hc % 2
                        sc.op("sync", lambda e_, hc=hc, bf=bf:
                              e_.dma_start(out=wg_t[:, bf * DM:(bf + 1) * DM], in_=wgT_d[hc]),
                              wait=[("pe", freed[f"wg{bf}"])], inc16="wld")
                        sc.op("sync", lambda e_, hc=hc, bf=bf:
                              e_.dma_start(out=wu_t[:, bf * DM:(bf + 1) * DM], in_=wuT_d[hc]),
                              wait=[("pe", freed[f"wu{bf}"])], inc16="wld")
                        wld_hc = sc.cnt["wld"]
                        for dc in range(DC):
                            w_ = None
                            if dc == 0:
                                w_ = [("wld", wld_hc), ("dve", dve_xln),
                                      ("act", freed["gate_ev"])]
                            sc.op("tensor", lambda e_, dc=dc, bf=bf:
                                  e_.matmul(ps_g[:],
                                            wg_t[:, bf * DM + dc * 128: bf * DM + (dc + 1) * 128],
                                            xln[:, dc * TQ:(dc + 1) * TQ],
                                            start=(dc == 0), stop=(dc == DC - 1)),
                                  wait=w_, inc="pe" if dc == DC - 1 else None)
                        pe_gate = sc.cnt["pe"]
                        freed[f"wg{bf}"] = pe_gate
                        for dc in range(DC):
                            w_ = [("act", freed["up_ev"])] if dc == 0 else None
                            sc.op("tensor", lambda e_, dc=dc, bf=bf:
                                  e_.matmul(ps_u[:],
                                            wu_t[:, bf * DM + dc * 128: bf * DM + (dc + 1) * 128],
                                            xln[:, dc * TQ:(dc + 1) * TQ],
                                            start=(dc == 0), stop=(dc == DC - 1)),
                                  wait=w_, inc="pe" if dc == DC - 1 else None)
                        pe_up = sc.cnt["pe"]
                        freed[f"wu{bf}"] = pe_up
                        dve_gprev = sc.cnt["dve"]
                        sc.op("scalar", lambda e_: e_.activation(gate_s[:], ps_g[:], AF.Silu),
                              wait=[("pe", pe_gate), ("dve", dve_gprev)], inc="act")
                        freed["gate_ev"] = sc.cnt["act"]
                        sc.op("scalar", lambda e_: e_.copy(up_s[:], ps_u[:]),
                              wait=[("pe", pe_up)], inc="act")
                        freed["up_ev"] = sc.cnt["act"]
                        act_ev = sc.cnt["act"]
                        mw = [("act", act_ev), ("pe", sc.cnt["pe"])]
                        if DEBUG_FFN and tag == "f1" and q == 0 and g == 1 and hcg == 0:
                            mw.append(("store", sc.cnt["store"]))
                        sc.op("vector", lambda e_, hcg=hcg:
                              e_.tensor_mul(gatedT[:, hcg * TQ:(hcg + 1) * TQ], gate_s[:], up_s[:]),
                              wait=mw, inc="dve")
                    dve_gated = sc.cnt["dve"]
                    if DEBUG_FFN and tag == "f1" and q == 0 and g == 0:
                        sc.op("scalar", lambda e_: e_.dma_start(out=dbg_gated[:, :],
                                                                in_=gatedT[:]),
                              wait=[("dve", dve_gated)], inc16="store")
                        dbg_store = sc.cnt["store"]
                    for dco in range(DC):
                        bf = dco % 2
                        sc.op("sync", lambda e_, dco=dco, g=g, bf=bf:
                              e_.dma_start(out=wd_t[:, bf * 2048:(bf + 1) * 2048],
                                           in_=wdT_d[dco, g]),
                              wait=[("pe", freed[f"wd{bf}"])], inc16="wld")
                        wld_wd = sc.cnt["wld"]
                        ps = ps_d[dco % 2]
                        for hcg in range(HCG):
                            w_ = None
                            if hcg == 0:
                                w_ = [("wld", wld_wd), ("dve", dve_gated),
                                      ("dve", freed[f"dps{dco % 2}"])]
                            sc.op("tensor", lambda e_, hcg=hcg, bf=bf, ps=ps:
                                  e_.matmul(ps[:],
                                            wd_t[:, bf * 2048 + hcg * 128: bf * 2048 + (hcg + 1) * 128],
                                            gatedT[:, hcg * TQ:(hcg + 1) * TQ],
                                            start=(hcg == 0), stop=(hcg == HCG - 1)),
                                  wait=w_, inc="pe" if hcg == HCG - 1 else None)
                        pe_down = sc.cnt["pe"]
                        freed[f"wd{bf}"] = pe_down
                        sc.op("vector", lambda e_, dco=dco, ps=ps, t0=t0:
                              e_.scalar_tensor_tensor(xs_t(dco)[:, t0:t0 + TQ], ps[:], 0.5,
                                                      xs_t(dco)[:, t0:t0 + TQ], ALU.mult, ALU.add),
                              wait=[("pe", pe_down)], inc="dve")
                        freed[f"dps{dco % 2}"] = sc.cnt["dve"]
        sc.barrier()
        sc.flush(block)

    if PHASE_LIMIT >= 1:
        emit_ffn("f1")

    store_qkv_end = 0
    if PHASE_LIMIT >= 2:
        with contextlib.ExitStack() as qst:
            qe = qst.enter_context
            xln2 = qe(nc.sbuf_tensor("xln2", [128, DC * TQ], BF16))
            cos_s = qe(nc.sbuf_tensor("cos_s", [128, T], F32))
            sin_s = qe(nc.sbuf_tensor("sin_s", [128, T], F32))
            qkv_t = qe(nc.sbuf_tensor("qkv_t", [128, 2 * DM], BF16))
            xcast2 = qe(nc.sbuf_tensor("xc2", [128, 2 * TQ], BF16))
            xsq2 = qe(nc.sbuf_tensor("xq2", [128, 2 * TQ], BF16))
            tmp2 = qe(nc.sbuf_tensor("tmp2", [128, TQ], F32))
            qf = qe(nc.sbuf_tensor("qf", [128, TQ], F32))
            qb2 = qe(nc.sbuf_tensor("qb2", [128, TQ], BF16))
            qq2 = qe(nc.sbuf_tensor("qq2", [128, TQ], BF16))
            qn_t = qe(nc.sbuf_tensor("qn_t", [128, TQ], F32))
            qr1 = qe(nc.sbuf_tensor("qr1", [128, TQ], F32))
            qr2 = qe(nc.sbuf_tensor("qr2", [128, TQ], F32))
            qro = qe(nc.sbuf_tensor("qro", [128, TQ], BF16))
            qsw = qe(nc.sbuf_tensor("qsw", [128, TQ], F32))
            vtm = qe(nc.sbuf_tensor("vtm", [128, TQ], BF16))
            vec2 = qe(nc.sbuf_tensor("vec2", [1, 5 * TQ], F32))
            ps_q1 = qe(nc.psum_tensor("ps_q1", [128, TQ], F32))
            ps_q2 = qe(nc.psum_tensor("ps_q2", [128, TQ], F32))
            ps_t1 = qe(nc.psum_tensor("ps_t1", [1, TQ], F32))
            ps_t2 = qe(nc.psum_tensor("ps_t2", [1, TQ], F32))
            ps_b1 = qe(nc.psum_tensor("ps_b1", [128, TQ], F32))
            ps_b2 = qe(nc.psum_tensor("ps_b2", [128, TQ], F32))
            ps_vt = qe(nc.psum_tensor("ps_vt", [128, TQ], BF16))

            vsl2 = tuple(vec2[:, i * TQ:(i + 1) * TQ] for i in range(5))
            V2_MU, V2_M2, V2_VAR, V2_RSTD, V2_NMR = vsl2
            ps_q = [ps_q1, ps_q2]
            freed = {"qkv0": 0, "qkv1": 0, "qev0": 0, "qev1": 0}

            sc.op("sync", lambda e_: e_.dma_start(out=cos_s[:], in_=cos_fm[:, :]), inc16="ld")
            sc.op("sync", lambda e_: e_.dma_start(out=sin_s[:], in_=sin_fm[:, :]), inc16="ld")
            ld_rope = sc.cnt["ld"]

            for q in range(NQ):
                t0 = q * TQ
                dve_xln2 = emit_stream_ln(t0, xln2, xcast2, xsq2, tmp2, vsl2,
                                          ps_b1, ps_b2, ps_t1, ps_t2)
                for oc in range(OC):
                    bf = oc % 2
                    ps = ps_q[oc % 2]
                    sc.op("sync", lambda e_, oc=oc, bf=bf:
                          e_.dma_start(out=qkv_t[:, bf * DM:(bf + 1) * DM], in_=qkvT_d[oc]),
                          wait=[("pe", freed[f"qkv{bf}"])], inc16="wld")
                    wld_oc = sc.cnt["wld"]
                    for dc in range(DC):
                        w_ = None
                        if dc == 0:
                            w_ = [("wld", wld_oc), ("dve", dve_xln2),
                                  ("act", freed[f"qev{oc % 2}"])]
                        sc.op("tensor", lambda e_, dc=dc, bf=bf, ps=ps:
                              e_.matmul(ps[:],
                                        qkv_t[:, bf * DM + dc * 128: bf * DM + (dc + 1) * 128],
                                        xln2[:, dc * TQ:(dc + 1) * TQ],
                                        start=(dc == 0), stop=(dc == DC - 1)),
                              wait=w_, inc="pe" if dc == DC - 1 else None)
                    pe_qkv = sc.cnt["pe"]
                    freed[f"qkv{bf}"] = pe_qkv

                    if oc < NHEAD + NKV:
                        is_q = oc < NHEAD
                        gg, bb = (qng_s, qnb_s) if is_q else (kng_s, knb_s)
                        dve_pre = sc.cnt["dve"]
                        sc.op("scalar", lambda e_, ps=ps: e_.copy(qf[:], ps[:]),
                              wait=[("pe", pe_qkv), ("dve", dve_pre),
                                    ("pool", sc.cnt["pool"])], inc="act")
                        freed[f"qev{oc % 2}"] = sc.cnt["act"]
                        act_e = sc.cnt["act"]
                        sc.op("gpsimd", lambda e_: e_.tensor_copy(qb2[:], qf[:]),
                              wait=[("act", act_e), ("pe", sc.cnt["pe"])], inc="pool")
                        sc.op("scalar", lambda e_: e_.activation(qq2[:], qf[:], AF.Square),
                              inc="act")
                        sc.op("tensor", lambda e_: e_.matmul(ps_t1[:], ones_b[:], qb2[:],
                                                             start=True, stop=True),
                              wait=[("pool", sc.cnt["pool"]), ("dve", sc.cnt["dve"])])
                        sc.op("tensor", lambda e_: e_.matmul(ps_t2[:], ones_b[:], qq2[:],
                                                             start=True, stop=True),
                              wait=[("act", sc.cnt["act"])], inc="pe")
                        pe_st = sc.cnt["pe"]
                        sc.op("vector", lambda e_: e_.tensor_scalar_mul(V2_MU, ps_t1[:], 1.0 / HD),
                              wait=[("pe", pe_st)], inc="dve")
                        sc.op("vector", lambda e_: e_.tensor_scalar_mul(V2_M2, ps_t2[:], 1.0 / HD),
                              inc="dve")
                        sc.op("vector", lambda e_: e_.tensor_mul(V2_VAR, V2_MU, V2_MU), inc="dve")
                        sc.op("vector", lambda e_: e_.tensor_sub(V2_VAR, V2_M2, V2_VAR), inc="dve")
                        sc.op("vector", lambda e_: e_.tensor_scalar_add(V2_VAR, V2_VAR, EPS),
                              inc="dve")
                        sc.op("scalar", lambda e_: e_.activation(V2_RSTD, V2_VAR, AF.Ln),
                              wait=[("dve", sc.cnt["dve"])], inc="act")
                        sc.op("scalar", lambda e_: e_.activation(V2_RSTD, V2_RSTD, AF.Exp, scale=-0.5),
                              inc="act")
                        a_s = sc.cnt["act"]
                        sc.op("vector", lambda e_: e_.tensor_mul(V2_NMR, V2_MU, V2_RSTD),
                              wait=[("act", a_s)], inc="dve")
                        sc.op("vector", lambda e_: e_.tensor_scalar_mul(V2_NMR, V2_NMR, -1.0),
                              inc="dve")
                        d_s = sc.cnt["dve"]
                        sc.op("tensor", lambda e_: e_.matmul(ps_b1[:], ones_f[:], V2_RSTD,
                                                             start=True, stop=True),
                              wait=[("dve", d_s)])
                        sc.op("tensor", lambda e_: e_.matmul(ps_b2[:], ones_f[:], V2_NMR,
                                                             start=True, stop=True), inc="pe")
                        pe_b = sc.cnt["pe"]
                        sc.op("vector", lambda e_: e_.tensor_mul(tmp2[:], qf[:], ps_b1[:]),
                              wait=[("pe", pe_b)], inc="dve")
                        sc.op("vector", lambda e_: e_.tensor_add(tmp2[:], tmp2[:], ps_b2[:]),
                              inc="dve")
                        sc.op("vector", lambda e_, gg=gg, bb=bb:
                              e_.tensor_scalar(qn_t[:], tmp2[:], gg[:], bb[:],
                                               ALU.mult, ALU.add),
                              inc="dve")
                        dve_qn = sc.cnt["dve"]
                        sc.op("scalar", lambda e_: e_.dma_start(out=qsw[0:64, :],
                                                                in_=qn_t[64:128, :]),
                              wait=[("dve", dve_qn)], inc16="swp")
                        sc.op("scalar", lambda e_: e_.dma_start(out=qsw[64:128, :],
                                                                in_=qn_t[0:64, :]),
                              inc16="swp")
                        swp_ro = sc.cnt["swp"]
                        sc.op("vector", lambda e_, t0=t0:
                              e_.tensor_mul(qr1[:], qn_t[:], cos_s[:, t0:t0 + TQ]),
                              wait=[("ld", ld_rope)], inc="dve")
                        sc.op("vector", lambda e_, t0=t0:
                              e_.tensor_mul(qr2[:], qsw[:], sin_s[:, t0:t0 + TQ]),
                              wait=[("swp", swp_ro)], inc="dve")
                        sc.op("vector", lambda e_: e_.tensor_add(qro[:], qr1[:], qr2[:]),
                              wait=[("store", sc.cnt["store"])], inc="dve")
                        dve_ro = sc.cnt["dve"]
                        if is_q:
                            sc.op("scalar", lambda e_, oc=oc, t0=t0:
                                  e_.dma_start(out=q_d[oc, :, t0:t0 + TQ], in_=qro[:]),
                                  wait=[("dve", dve_ro)], inc16="store")
                        else:
                            kv = oc - NHEAD
                            sc.op("scalar", lambda e_, kv=kv, t0=t0:
                                  e_.dma_start(out=kex_in[kv * HD:(kv + 1) * HD, t0:t0 + TQ],
                                               in_=qro[:]),
                                  wait=[("dve", dve_ro)], inc16="store")
                    else:
                        kv = oc - NHEAD - NKV
                        dve_pre = sc.cnt["dve"]
                        sc.op("scalar", lambda e_, ps=ps: e_.copy(vtm[:], ps[:]),
                              wait=[("pe", pe_qkv), ("dve", dve_pre), ("store", sc.cnt["store"])],
                              inc="act")
                        freed[f"qev{oc % 2}"] = sc.cnt["act"]
                        act_v = sc.cnt["act"]
                        for j in range(4):
                            sc.op("tensor", lambda e_, j=j:
                                  e_.transpose(ps_vt[:, j * 128:(j + 1) * 128],
                                               vtm[:, j * 128:(j + 1) * 128], ident_b[:]),
                                  wait=[("act", act_v), ("dve", sc.cnt["dve"])] if j == 0 else None,
                                  inc="pe" if j == 3 else None)
                        pe_vt = sc.cnt["pe"]
                        sc.op("vector", lambda e_: e_.tensor_copy(vtm[:], ps_vt[:]),
                              wait=[("pe", pe_vt), ("act", sc.cnt["act"])], inc="dve")
                        dve_v = sc.cnt["dve"]
                        sc.op("scalar", lambda e_, kv=kv, t0=t0:
                              e_.dma_start(
                                  out=vex_in.rearrange("(kv tc p) e -> kv p tc e", kv=NKV, p=128)
                                      [kv][:, t0 // 128: t0 // 128 + 4, :],
                                  in_=vtm.rearrange("p (tc e) -> p tc e", e=128)),
                              wait=[("dve", dve_v)], inc16="store")
            store_qkv_end = sc.cnt["store"]
            sc.barrier()
            sc.flush(block)

    if PHASE_LIMIT >= 3:
        sc.wait("gpsimd", "store", store_qkv_end)
        sc.op("gpsimd", lambda e_: e_.collective_compute(
            "AllGather", ALU.bypass,
            replica_groups=[[0, 1], [2, 3], [4, 5], [6, 7]],
            ins=[kex_in[:, :]], outs=[kex_out[:, :]]), inc="cc")
        sc.op("gpsimd", lambda e_: e_.collective_compute(
            "AllGather", ALU.bypass,
            replica_groups=[[0, 1], [2, 3], [4, 5], [6, 7]],
            ins=[vex_in[:, :]], outs=[vex_out[:, :]]), inc="cc")
        sc.flush(block)

        with contextlib.ExitStack() as ast_:
            ae = ast_.enter_context
            KVg = ae(nc.sbuf_tensor("KVg", [128, 4 * T], BF16))
            Ssb = ae(nc.sbuf_tensor("Ssb", [128, 2 * T], F32))
            Psb = ae(nc.sbuf_tensor("Psb", [128, 8 * 512], BF16))
            PTs = ae(nc.sbuf_tensor("PTs", [128, 2 * 512], BF16))
            avT = ae(nc.sbuf_tensor("avT", [128, 4 * T], BF16))
            woT_s = ae(nc.sbuf_tensor("woT_s", [128, 4 * DM], BF16))
            Qh = ae(nc.sbuf_tensor("Qh", [128, 2 * 128], BF16))
            avsb = ae(nc.sbuf_tensor("avsb", [128, 2 * 128], BF16))
            mrow = ae(nc.sbuf_tensor("mrow", [128, 64], F32))
            ps_S1 = ae(nc.psum_tensor("ps_S1", [128, 512], F32))
            ps_S2 = ae(nc.psum_tensor("ps_S2", [128, 512], F32))
            ps_PT1 = ae(nc.psum_tensor("ps_PT1", [128, 512], BF16))
            ps_PT2 = ae(nc.psum_tensor("ps_PT2", [128, 512], BF16))
            ps_AV1 = ae(nc.psum_tensor("ps_AV1", [128, 128], F32))
            ps_AV2 = ae(nc.psum_tensor("ps_AV2", [128, 128], F32))
            ps_W1 = ae(nc.psum_tensor("ps_W1", [128, TQ], F32))
            ps_W2 = ae(nc.psum_tensor("ps_W2", [128, TQ], F32))

            Kp = KVg[:, 0:2 * T]
            Vp = KVg[:, 2 * T:4 * T]
            ps_S = [ps_S1, ps_S2]
            ps_PT = [ps_PT1, ps_PT2]
            ps_AVp = [ps_AV1, ps_AV2]
            ps_W = [ps_W1, ps_W2]
            NM = [mrow[:, 0:1], mrow[:, 1:2]]
            DNv = [mrow[:, 2:3], mrow[:, 3:4]]
            RCv = [mrow[:, 4:5], mrow[:, 5:6]]
            SPC = mrow[:, 6:7]
            SPC2 = mrow[:, 7:8]
            DNs = [mrow[:, 32:40], mrow[:, 40:48]]
            mask_t = ae(nc.sbuf_tensor("mask_t", [128, 256], F32))

            sc.op("gpsimd", lambda e_: e_.memset(mrow[:], 0.0), inc="pool")
            sc.op("gpsimd", lambda e_: e_.memset(Ssb[:], 0.0), inc="pool")
            sc.op("gpsimd", lambda e_: e_.memset(Psb[:], 0.0), inc="pool")
            sc.op("gpsimd", lambda e_: e_.iota(mask_t[:], [[1, 256]], base=0,
                                               channel_multiplier=0,
                                               allow_small_or_imprecise_dtypes=True),
                  inc="pool")
            pool0 = sc.cnt["pool"]
            sc.wait("scalar", "pool", pool0)
            sc.op("vector", lambda e_:
                  e_.tensor_scalar(mask_t[:], mask_t[:], qsh_s[:], float(NEG),
                                   ALU.is_gt, ALU.mult),
                  wait=[("pool", pool0)], inc="dve")
            dve_me = sc.cnt["dve"]

            kex_v = kex_out.rearrange("(half kv p) t -> half kv p t", half=2, p=128)
            vex_v = vex_out.rearrange("(half kv tc p) e -> half kv tc p e",
                                      half=2, kv=NKV, p=128)

            units = []
            for g in range(NKV):
                for hh in range(4):
                    for tb in range(NB):
                        units.append((g, hh, tb, 256 * (tb + 1)))
            if ATT_UNITS is not None:
                units = units[:ATT_UNITS]
            NU = len(units)

            pe_S_end = {}
            pe_AV_end = {}
            pe_AT = {}
            pe_PTs = {}
            dve_PTc = {}
            act_exp = {}
            act_dn = {}
            dve_nm = {}
            dve_dn = {}
            dve_scale = {}
            ssb_act = [0] * 8
            fr_pt = [0, 0]
            fr_pts = [0, 0]
            fr_psb = [0] * 8
            fr_av = [0, 0]
            fr_s = [0, 0]
            fr_qh = [0, 0]
            fr_W = [0, 0]
            ld_qh = {}
            ld_kp = {}
            ld_vp = {}
            ld_wo = {}
            dve_avT_g = {}
            wo_block_pe = {}
            s_bank = [0]
            wo_i = [0]
            sec_pe = {}

            def secs(W):
                out = []
                j = 0
                w0 = 0
                while w0 < W:
                    w1 = min(W, w0 + 512)
                    out.append((j, w0, w1))
                    j += 1
                    w0 = w1
                return out

            def emit_kp_loads(g, pe_gate):
                w_l = [("cc", 1), ("store", store_qkv_end)]
                if pe_gate:
                    w_l.append(("pe", pe_gate))
                for j in range(16):
                    sc.op("sync", lambda e_, g=g, j=j: e_.dma_start(
                        out=Kp.rearrange("p (j half k) -> p j half k", half=2, k=128)[:, j],
                        in_=kex_v[:, g].rearrange("half p (j k) -> p j half k", k=128)[:, j],
                        allow_slow_non_contiguous=True),
                        wait=w_l if j == 0 else None, inc16="ld")
                ld_kp[g] = sc.cnt["ld"]

            def emit_vp_loads(g, pe_gate):
                w_l = [("cc", 2), ("store", store_qkv_end)]
                if pe_gate:
                    w_l.append(("pe", pe_gate))
                for tc in range(16):
                    sc.op("sync", lambda e_, g=g, tc=tc: e_.dma_start(
                        out=Vp.rearrange("p (tc half e) -> p tc half e", half=2, e=128)[:, tc],
                        in_=vex_v[:, g].rearrange("half tc p e -> p tc half e")[:, tc],
                        allow_slow_non_contiguous=True),
                        wait=w_l if tc == 0 else None, inc16="ld")
                ld_vp[g] = sc.cnt["ld"]

            def emit_wo_loads(g, pe_gate):
                for j in range(4):
                    w_l = [("pe", pe_gate)] if (j == 0 and pe_gate) else None
                    sc.op("sync", lambda e_, g=g, j=j: e_.dma_start(
                        out=woT_s[:, j * DM:(j + 1) * DM],
                        in_=woT_d[(4 * g + j) * 128:(4 * g + j + 1) * 128, :]),
                        wait=w_l, inc16="ld")
                ld_wo[g] = sc.cnt["ld"]

            def emit_wo_block(gb, c):
                adds = []

                def emit_add(rec):
                    dco, pw, pe_c = rec
                    sc.op("vector", lambda e_, dco=dco, pw=pw, c=c:
                          e_.tensor_add(xs_t(dco)[:, c * TQ:(c + 1) * TQ],
                                        xs_t(dco)[:, c * TQ:(c + 1) * TQ], ps_W[pw][:]),
                          wait=[("pe", pe_c)], inc="dve")
                    fr_W[pw] = sc.cnt["dve"]

                for dco in range(DC):
                    pw = wo_i[0] % 2
                    wo_i[0] += 1
                    for j in range(4):
                        w_l = None
                        if j == 0:
                            w_l = [("dve", fr_W[pw]), ("act", dve_avT_g[gb]),
                                   ("ld", ld_wo[gb])]
                        sc.op("tensor", lambda e_, pw=pw, j=j, dco=dco, c=c:
                              e_.matmul(ps_W[pw][:],
                                        woT_s[:, j * DM + dco * 128: j * DM + (dco + 1) * 128],
                                        avT[:, j * T + c * TQ: j * T + (c + 1) * TQ],
                                        start=(j == 0), stop=(j == 3)),
                              wait=w_l, inc="pe" if j == 3 else None)
                    adds.append((dco, pw, sc.cnt["pe"]))
                    if len(adds) >= 2:
                        emit_add(adds.pop(0))
                emit_add(adds.pop(0))
                wo_block_pe[(gb, c)] = sc.cnt["pe"]

            emit_wo_loads(0, 0)

            for it in range(NU + 2):
                u = it if it < NU else None
                v = it - 1 if 0 <= it - 1 < NU else None
                w = it - 2 if 0 <= it - 2 < NU else None

                if u is not None:
                    g, hh, tb, W = units[u]
                    h = 4 * g + hh
                    if hh == 0 and tb == 0:
                        gate = pe_S_end[u - 1] if u > 0 else 0
                        emit_kp_loads(g, gate)
                    sc.op("sync", lambda e_, h=h, tb=tb, u=u: e_.dma_start(
                        out=Qh[:, (u % 2) * 128:(u % 2) * 128 + 128],
                        in_=q_d[h][:, tb * 128:(tb + 1) * 128]),
                        wait=[("pe", fr_qh[u % 2]), ("store", store_qkv_end)],
                        inc16="ld")
                    ld_qh[u] = sc.cnt["ld"]
                    if ATT_WO and hh == 0 and tb in (2, 6, 10, 14) and g > 0:
                        emit_wo_block(g - 1, (tb - 2) // 4)
                        if tb == 14:
                            emit_wo_loads(g, wo_block_pe[(g - 1, 3)])

                if u is not None:
                    g, hh, tb, W = units[u]
                    h = 4 * g + hh
                    slist = secs(W)
                    nblk = len(slist)
                    for (j, w0, w1) in slist:
                        sb = s_bank[0] % 2
                        s_bank[0] += 1
                        ln = w1 - w0
                        w_l = [("dve", fr_s[sb])]
                        if j == 0:
                            w_l += [("ld", ld_qh[u]), ("ld", ld_kp[g])]
                        sc.op("tensor", lambda e_, sb=sb, w0=w0, w1=w1, u=u:
                              e_.matmul(ps_S[sb][:, 0:w1 - w0],
                                        Qh[:, (u % 2) * 128:(u % 2) * 128 + 128],
                                        Kp[:, w0:w1], start=True, stop=True),
                              wait=w_l, inc="pe")
                        sec_pe[(u, j)] = sc.cnt["pe"]
                        ev_l = [("pe", sec_pe[(u, j)]), ("act", ssb_act[j])]
                        if u == 0 and j == 0:
                            ev_l.append(("dve", dve_me))
                        if j < nblk - 1:
                            sc.op("vector", lambda e_, sb=sb, w0=w0, w1=w1:
                                  e_.tensor_copy(Ssb[:, w0:w1], ps_S[sb][:, 0:w1 - w0]),
                                  wait=ev_l, inc="dve")
                        else:
                            if ln == 512:
                                sc.op("vector", lambda e_, sb=sb, w0=w0:
                                      e_.tensor_copy(Ssb[:, w0:w0 + 256],
                                                     ps_S[sb][:, 0:256]),
                                      wait=ev_l, inc="dve")
                                ev_l = None
                            sc.op("vector", lambda e_, sb=sb, w1=w1, ln=ln:
                                  e_.tensor_add(Ssb[:, w1 - 256:w1],
                                                ps_S[sb][:, ln - 256:ln], mask_t[:]),
                                  wait=ev_l, inc="dve")
                        fr_s[sb] = sc.cnt["dve"]
                    pe_S_end[u] = sc.cnt["pe"]
                    fr_qh[u % 2] = sc.cnt["pe"]
                    sc.op("vector", lambda e_: e_.tensor_scalar_add(SPC, qsh_s[:], 0.0),
                          inc="dve")
                    sc.op("vector", lambda e_, u=u, W=W:
                          e_.reduce_max(NM[u % 2], Ssb[:, 0:W],
                                        axis=AX.X, negate=True), inc="dve")
                    sc.op("vector", lambda e_: e_.tensor_scalar_add(SPC, qsh_s[:], 0.0),
                          inc="dve")
                    dve_nm[u] = sc.cnt["dve"]

                if w is not None:
                    gw, hw, tw, Ww = units[w]
                    b_at = len(secs(Ww)) % 2
                    sc.op("tensor", lambda e_, b_at=b_at, w=w:
                          e_.transpose(ps_PT[b_at][:, 0:128],
                                       avsb[:, (w % 2) * 128:(w % 2) * 128 + 128],
                                       ident_b[:]),
                          wait=[("dve", dve_scale[w]), ("act", fr_pt[b_at])], inc="pe")
                    pe_AT[w] = sc.cnt["pe"]

                if w is not None:
                    gw, hw, tw, Ww = units[w]
                    b_at = len(secs(Ww)) % 2
                    w_l = [("pe", pe_AT[w])]
                    if ATT_WO and hw == 0 and gw > 0:
                        w_l.append(("pe", wo_block_pe[(gw - 1, tw // 4)]))
                    sc.op("scalar", lambda e_, hw=hw, tw=tw, b_at=b_at:
                          e_.copy(avT[:, hw * T + tw * 128: hw * T + (tw + 1) * 128],
                                  ps_PT[b_at][:, 0:128]),
                          wait=w_l, inc="act")
                    fr_pt[b_at] = sc.cnt["act"]
                    dve_avT_g[gw] = sc.cnt["act"]

                if v is not None:
                    gv, hv, tv, Wv = units[v]
                    vlist = secs(Wv)
                    nkcW = Wv // 128
                    for (j, w0, w1) in vlist:
                        bb = j % 2
                        ln = w1 - w0
                        nkc = ln // 128
                        for i_ in range(nkc):
                            w_l = None
                            if i_ == 0:
                                w_l = [("act", act_exp[(v, j)]), ("act", fr_pt[bb])]
                            sc.op("tensor", lambda e_, bb=bb, i_=i_, j=j:
                                  e_.transpose(ps_PT[bb][:, i_ * 128:(i_ + 1) * 128],
                                               Psb[:, j * 512 + i_ * 128: j * 512 + (i_ + 1) * 128],
                                               ident_b[:]),
                                  wait=w_l, inc="pe" if i_ == nkc - 1 else None)
                        pe_PTs[(v, j)] = sc.cnt["pe"]
                        fr_psb[j] = sc.cnt["pe"]
                        sc.op("scalar", lambda e_, bb=bb, ln=ln:
                              e_.copy(PTs[:, bb * 512: bb * 512 + ln],
                                      ps_PT[bb][:, 0:ln]),
                              wait=[("pe", pe_PTs[(v, j)]), ("pe", fr_pts[bb])],
                              inc="act")
                        dve_PTc[(v, j)] = sc.cnt["act"]
                        fr_pt[bb] = sc.cnt["act"]
                        for i_ in range(nkc):
                            kc = 4 * j + i_
                            w_l = None
                            if i_ == 0:
                                w_l = [("act", dve_PTc[(v, j)])]
                                if kc == 0:
                                    w_l += [("dve", fr_av[v % 2]),
                                            ("ld", ld_vp[gv])]
                            sc.op("tensor", lambda e_, bb=bb, i_=i_, kc=kc, v=v,
                                  first=(kc == 0), last=(kc == nkcW - 1):
                                  e_.matmul(ps_AVp[v % 2][:],
                                            PTs[:, bb * 512 + i_ * 128: bb * 512 + (i_ + 1) * 128],
                                            Vp[:, kc * 128:(kc + 1) * 128],
                                            start=first, stop=last, skip_group_check=True),
                                  wait=w_l, inc="pe" if i_ == nkc - 1 else None)
                        fr_pts[bb] = sc.cnt["pe"]
                    pe_AV_end[v] = sc.cnt["pe"]

                if u is not None:
                    g, hh, tb, W = units[u]
                    slist = secs(W)
                    for (j, w0, w1) in slist:
                        w_l = [("pe", fr_psb[j])]
                        if j == 0:
                            w_l.append(("dve", dve_nm[u]))
                            if u >= 2:
                                w_l.append(("dve", dve_dn[u - 2]))
                        sc.op("scalar", lambda e_, w0=w0, w1=w1, u=u, j=j:
                              e_.activation(Psb[:, j * 512: j * 512 + (w1 - w0)],
                                            Ssb[:, w0:w1], AF.Exp, bias=NM[u % 2],
                                            accum_out=DNs[u % 2][:, j:j + 1]),
                              wait=w_l, inc="act")
                        act_exp[(u, j)] = sc.cnt["act"]
                        ssb_act[j] = sc.cnt["act"]
                    sc.op("scalar", lambda e_: e_.copy(SPC2, qsh_s[:]), inc="act")
                    act_dn[u] = sc.cnt["act"]

                if v is not None:
                    gv, hv, tv, Wv = units[v]
                    nblk = len(secs(Wv))
                    sc.op("vector", lambda e_, v=v, nblk=nblk:
                          e_.reduce_sum(DNv[v % 2], DNs[v % 2][:, 0:nblk], axis=AX.X),
                          wait=[("act", act_dn[v])], inc="dve")
                    sc.op("vector", lambda e_: e_.tensor_scalar_add(SPC, qsh_s[:], 0.0),
                          inc="dve")
                    dve_dn[v] = sc.cnt["dve"]
                    sc.op("vector", lambda e_, v=v: e_.reciprocal(RCv[v % 2], DNv[v % 2]),
                          inc="dve")
                    sc.op("vector", lambda e_: e_.tensor_scalar_add(SPC, qsh_s[:], 0.0),
                          inc="dve")
                    sc.op("vector", lambda e_, v=v:
                          e_.tensor_scalar_mul(avsb[:, (v % 2) * 128:(v % 2) * 128 + 128],
                                               ps_AVp[v % 2][:], RCv[v % 2]),
                          wait=[("pe", pe_AV_end[v])], inc="dve")
                    dve_scale[v] = sc.cnt["dve"]
                    fr_av[v % 2] = sc.cnt["dve"]
                if v is not None and units[v][1] == 3 and units[v][2] == NB - 1:
                    gnext = units[v][0] + 1
                    if gnext < NKV:
                        emit_vp_loads(gnext, pe_AV_end[v])
                if it == 0:
                    emit_vp_loads(0, 0)

            if ATT_WO and ATT_UNITS is None:
                for c in range(4):
                    emit_wo_block(NKV - 1, c)
            sc.barrier()
            sc.flush(block)

    if PHASE_LIMIT >= 4:
        emit_ffn("f3")

    for dc in range(DC):
        w_ = None
        if dc == 0:
            w_ = [("dve", sc.cnt["dve"]), ("pe", sc.cnt["pe"]), ("act", sc.cnt["act"])]
        sc.op("sync", lambda e_, dc=dc: e_.dma_start(
            out=out_fm[dc * 128:(dc + 1) * 128, :], in_=xs_t(dc)),
            wait=w_, inc16="store")
    sc.wait("sync", "store", sc.cnt["store"])
    sc.flush(block)

    st.close()
    sc.stack.close()
    return nc


def _core_positions(s):
    blocks = [(2 * j + s) for j in range(NB)]
    return np.concatenate([np.arange(b * 128, (b + 1) * 128) for b in blocks])


def _rope_tables(pos):
    inv_freq = (1.0 / (ROPE_BASE ** (np.arange(0, HD, 2, dtype=np.float32) / HD))).astype(np.float32)
    freqs = np.outer(pos.astype(np.float32), inv_freq).astype(np.float32)
    cos = np.concatenate([np.cos(freqs), np.cos(freqs)], axis=-1)
    sin = np.concatenate([-np.sin(freqs), np.sin(freqs)], axis=-1)
    return (np.ascontiguousarray(cos.T.astype(np.float32)),
            np.ascontiguousarray(sin.T.astype(np.float32)))


def kernel(L0_attn_x, L0_attn_w_qkv, L0_attn_w_o, L0_attn_qn_g, L0_attn_qn_b,
           L0_attn_kn_g, L0_attn_kn_b, L0_ffn_w_gate, L0_ffn_w_up, L0_ffn_w_down):
    x = np.asarray(L0_attn_x, dtype=np.float32)
    ident = np.eye(128, dtype=np.float32)
    in_common = {
        "w_qkv": np.ascontiguousarray(np.asarray(L0_attn_w_qkv, np.float32)),
        "w_o": np.ascontiguousarray(np.asarray(L0_attn_w_o, np.float32)),
        "qn_g": np.asarray(L0_attn_qn_g, np.float32).reshape(HD, 1),
        "qn_b": np.asarray(L0_attn_qn_b, np.float32).reshape(HD, 1),
        "kn_g": np.asarray(L0_attn_kn_g, np.float32).reshape(HD, 1),
        "kn_b": np.asarray(L0_attn_kn_b, np.float32).reshape(HD, 1),
        "w_gate": np.ascontiguousarray(np.asarray(L0_ffn_w_gate, np.float32)),
        "w_up": np.ascontiguousarray(np.asarray(L0_ffn_w_up, np.float32)),
        "w_down": np.ascontiguousarray(np.asarray(L0_ffn_w_down, np.float32)),
        "ident_bf_i": ident.astype(ml_dtypes.bfloat16),
        "ident_f_i": ident,
        "kpos": np.ascontiguousarray(np.tile(np.arange(256, dtype=np.float32), (128, 1))),
    }
    in_maps = []
    poss = []
    for c in range(8):
        b, s = c // 2, c % 2
        pos = _core_positions(s)
        poss.append(pos)
        cos_t, sin_t = _rope_tables(pos)
        m = dict(in_common)
        m["x_fm"] = np.ascontiguousarray(x[b][pos].T)
        m["cos_fm"] = cos_t
        m["sin_fm"] = sin_t
        m["qpos"] = np.ascontiguousarray(pos.astype(np.float32).reshape(NB, 128).T)
        m["qsh"] = np.ascontiguousarray((128.0 * s + np.arange(128, dtype=np.float32)).reshape(128, 1))
        in_maps.append(m)

    nc = build_program()
    res = run_bass_kernel_spmd(nc, in_maps, core_ids=list(range(8)))

    out = np.empty((B, S, DM), dtype=np.float32)
    for c in range(8):
        b = c // 2
        out[b][poss[c]] = np.asarray(res.results[c]["out_fm"]).T
    return out
